# version 01
# baseline (speedup 1.0000x reference)
import numpy as np
from contextlib import ExitStack
import concourse.bass as bass
import concourse.mybir as mybir
from concourse.bass_utils import run_bass_kernel_spmd

F32 = mybir.dt.float32
AF = mybir.ActivationFunctionType
ALU = mybir.AluOpType
AX = mybir.AxisListType

D = 1024
SEQ = 4096
NB = 4
NSUB = 4
NT = 4
OWN = 2048
HALO = 128
GLOB = 64
NCH = 8

C_INV1024 = 0
C_EPS = 1
C_NEG1 = 2
C_ONE = 3

ENGS = ("sync", "tensor", "vector", "scalar", "gpsimd")


DMA_SEMS = ("const", "x0", "x1", "x2", "x3", "wc0", "wc1",
            "wr0", "wr1", "wr2", "st0", "st1", "dbgst")


class Plan:
    def __init__(self):
        self.ops = {e: [] for e in ENGS}
        self.ctr = {n: 0 for n in ("pe", "ve", "sc") + DMA_SEMS}
        self.lastw = {e: {} for e in ENGS}

    def emit(self, eng, fn, wait=(), inc=None):
        w = []
        for sem, val in wait:
            if not val:
                continue
            if val > self.lastw[eng].get(sem, 0):
                w.append((sem, val))
                self.lastw[eng][sem] = val
        amt = 0
        if inc:
            amt = 16 if inc in DMA_SEMS else 1
            self.ctr[inc] += amt
        self.ops[eng].append((tuple(w), fn, inc, amt))
        return self.ctr[inc] if inc else None


def build_program(dbg=False, nt=NT, dbg_it=None):
    nc = bass.Bass("TRN2", target_bir_lowering=False)

    xm = nc.dram_tensor("x_main", [HALO + OWN, D], F32, kind="ExternalInput").ap()
    xg_in = nc.dram_tensor("x_glob", [GLOB, D], F32, kind="ExternalInput").ap()
    wqkv = nc.dram_tensor("w_qkv", [D, 3 * D], F32, kind="ExternalInput").ap()
    wout = nc.dram_tensor("w_out", [D, D], F32, kind="ExternalInput").ap()
    masks_in = nc.dram_tensor("masks", [3, 128, 320], F32, kind="ExternalInput").ap()
    ident_in = nc.dram_tensor("ident", [128, 128], F32, kind="ExternalInput").ap()
    cpool_in = nc.dram_tensor("cpool", [128, 8], F32, kind="ExternalInput").ap()
    aer_in = nc.dram_tensor("aexp_r", [4, 512, 512], F32, kind="ExternalInput").ap()
    ber_in = nc.dram_tensor("bexp_r", [4, 512, 512], F32, kind="ExternalInput").ap()
    aei_in = nc.dram_tensor("aexp_i", [4, 512, 512], F32, kind="ExternalInput").ap()
    bei_in = nc.dram_tensor("bexp_i", [4, 512, 512], F32, kind="ExternalInput").ap()
    bias_in = nc.dram_tensor("fbias", [1, 1024], F32, kind="ExternalInput").ap()
    y = nc.dram_tensor("y", [OWN, D], F32, kind="ExternalOutput").ap()
    dbg_t = {}
    if dbg:
        for name, shape in (("dbg_qfm", [128, 4096]), ("dbg_kcur", [128, 4096]),
                            ("dbg_vcur", [4, 128, 1024]), ("dbg_kprev", [128, 1024]),
                            ("dbg_kglob", [128, 512]), ("dbg_vglob", [64, 1024]),
                            ("dbg_vprev", [128, 1024]),
                            ("dbg_aot", [128, 4096]), ("dbg_ln1", [4, 128, 1024]),
                            ("dbg_wtr", [128, 2048]), ("dbg_wti", [128, 2048]),
                            ("dbg_xt", [128, 4096]),
                            ("dbg_stat", [4, 128, 8]), ("dbg_sq", [128, 1024]),
                            ("dbg_smax", [128, 1]), ("dbg_p1", [128, 320]),
                            ("dbg_p2", [128, 320]), ("dbg_p3", [128, 320]),
                            ("dbg_ssum", [128, 1]), ("dbg_pt", [128, 384])):
            dbg_t[name] = nc.dram_tensor(name, shape, F32, kind="ExternalOutput").ap()

    p = Plan()

    with ExitStack() as ctx:
        e = ctx.enter_context
        ident = e(nc.sbuf_tensor("ident_sb", [128, 128], F32))
        cpool = e(nc.sbuf_tensor("cpool_sb", [128, 8], F32))
        masks = e(nc.sbuf_tensor("masks_sb", [128, 3 * 320], F32))
        x_tm = [e(nc.sbuf_tensor(f"x_tm{i}", [128, D], F32)) for i in range(4)]
        xh_tm = x_tm[0]
        xg_tm = x_tm[1]
        xh_fm = e(nc.sbuf_tensor("xh_fm", [128, 1024], F32))
        xg_fm = e(nc.sbuf_tensor("xg_fm", [128, 512], F32))
        xT = e(nc.sbuf_tensor("xT", [128, 4096], F32))
        qfm = e(nc.sbuf_tensor("qfm", [128, 4096], F32))
        kcur = e(nc.sbuf_tensor("kcur", [128, 4096], F32))
        kprev = e(nc.sbuf_tensor("kprev", [128, 1024], F32))
        kglob = e(nc.sbuf_tensor("kglob", [128, 512], F32))
        vcur = [e(nc.sbuf_tensor(f"vcur{s}", [128, D], F32)) for s in range(4)]
        vprev = e(nc.sbuf_tensor("vprev", [128, D], F32))
        vglob = e(nc.sbuf_tensor("vglob", [64, D], F32))
        aoT = e(nc.sbuf_tensor("aoT", [128, 4096], F32))
        ln1tm = [e(nc.sbuf_tensor(f"ln1tm{s}", [128, D], F32)) for s in range(4)]
        otm = [e(nc.sbuf_tensor(f"otm{i}", [128, D], F32)) for i in range(2)]
        sqscr = xh_fm
        pexp = [e(nc.sbuf_tensor(f"pexp{i}", [128, 320], F32)) for i in range(2)]
        pT_sb = [e(nc.sbuf_tensor(f"pT{i}", [128, 384], F32)) for i in range(2)]
        wcol = [e(nc.sbuf_tensor(f"wcol{i}", [128, 1024], F32)) for i in range(2)]
        wrow = [e(nc.sbuf_tensor(f"wrow{i}", [128, 1024], F32)) for i in range(3)]
        wtr = e(nc.sbuf_tensor("wtr", [128, 2048], F32))
        wti = e(nc.sbuf_tensor("wti", [128, 2048], F32))
        wtn = e(nc.sbuf_tensor("wtn", [128, 2048], F32))
        fbias = e(nc.sbuf_tensor("fbias_sb", [1, 1024], F32))
        ones1 = e(nc.sbuf_tensor("ones1", [1, 128], F32))
        stat = [e(nc.sbuf_tensor(f"stat{i}", [128, 8], F32)) for i in range(4)]
        s_sum = [e(nc.sbuf_tensor(f"s_sum{i}", [128, 1], F32)) for i in range(4)]
        s_sq = [e(nc.sbuf_tensor(f"s_sq{i}", [128, 1], F32)) for i in range(4)]
        s_mean = [e(nc.sbuf_tensor(f"s_mean{i}", [128, 1], F32)) for i in range(4)]
        s_msq = [e(nc.sbuf_tensor(f"s_msq{i}", [128, 1], F32)) for i in range(4)]
        s_var = [e(nc.sbuf_tensor(f"s_var{i}", [128, 1], F32)) for i in range(4)]
        s_sd = [e(nc.sbuf_tensor(f"s_sd{i}", [128, 1], F32)) for i in range(4)]
        s_rs = [e(nc.sbuf_tensor(f"s_rs{i}", [128, 1], F32)) for i in range(4)]
        s_sh = [e(nc.sbuf_tensor(f"s_sh{i}", [128, 1], F32)) for i in range(4)]
        smax = [e(nc.sbuf_tensor(f"smax{i}", [128, 1], F32)) for i in range(2)]
        ssum = [e(nc.sbuf_tensor(f"ssum{i}", [128, 1], F32)) for i in range(2)]
        srinv = [e(nc.sbuf_tensor(f"srinv{i}", [128, 1], F32)) for i in range(2)]
        prodscr = vcur[0]
        dummy1 = e(nc.sbuf_tensor("dummy1", [1, 1], F32))

        pb = [e(nc.psum_tensor(f"pb{i}", [128, 512], F32)) for i in range(8)]
        bankfree = {i: None for i in range(8)}

        sems = {n: e(nc.semaphore(n)) for n in ("pe", "ve", "sc") + DMA_SEMS}

        def dma(dst, src, sem, wait=()):
            return p.emit("sync", lambda E, d=dst, s=src: E.dma_start(out=d, in_=s),
                          wait=wait, inc=sem)

        def store(dst, src, sem="dbgst", wait=()):
            return p.emit("gpsimd", lambda E, d=dst, s=src: E.dma_start(out=d, in_=s),
                          wait=wait, inc=sem)

        def pe(fn, wait=()):
            return p.emit("tensor", fn, wait=wait, inc="pe")

        def ve(fn, wait=()):
            return p.emit("vector", fn, wait=wait, inc="ve")

        def sc(fn, wait=()):
            return p.emit("scalar", fn, wait=wait, inc="sc")

        def bw(i):
            return [bankfree[i]] if bankfree[i] else []

        dma(ident[:], ident_in, "const")
        dma(cpool[:], cpool_in, "const")
        for v in range(3):
            dma(masks[:, 320 * v:320 * (v + 1)], masks_in[v], "const")
        dma(xh_tm[:], xm[0:HALO, :], "const")
        dma(xg_tm[0:64, :], xg_in, "const")
        dma(fbias[:], bias_in, "const")
        T_CONST = p.ctr["const"]
        t_ident = t_cpool = t_xh = t_xg = t_bias = T_CONST
        t_masks = [T_CONST] * 3
        ve(lambda E: E.memset(ones1[:], 1.0))

        def wcol_src(m):
            return bass.AP(wqkv.tensor, 128 * m,
                           [[3072, 128], [128 * 3072, 8], [1, 128]])

        def fm_transpose_pro(src_tm, dst, n_tok, wait_extra):
            tv = None
            for g in range(2):
                bank = 6 + g
                tk = None
                for c in range(4 * g, 4 * g + 4):
                    tk = pe(lambda E, c=c, bank=bank: E.transpose(
                        pb[bank][:, n_tok * (c % 4):n_tok * (c % 4) + n_tok],
                        src_tm[0:n_tok, 128 * c:128 * c + 128],
                        ident[0:n_tok, 0:n_tok]),
                        wait=wait_extra + bw(bank))
                tv = ve(lambda E, g=g, bank=bank: E.tensor_copy(
                    dst[:, 4 * n_tok * g:4 * n_tok * (g + 1)],
                    pb[bank][:, 0:4 * n_tok]),
                    wait=[("pe", tk)])
                bankfree[bank] = ("ve", tv)
            return tv

        t_xhfm = fm_transpose_pro(xh_tm, xh_fm, 128, [("const", T_CONST)])
        t_xgfm = fm_transpose_pro(xg_tm, xg_fm, 64, [("const", T_CONST)])

        wcol_used = {}
        t_kp = t_kg = None
        for mi, m in enumerate(range(8, 16)):
            buf = wcol[mi % 2]
            wsem = f"wc{mi % 2}"
            wb = [("pe", wcol_used[mi - 2])] if mi >= 2 else []
            tl = dma(buf[:], wcol_src(m), wsem, wait=wb)
            bank = mi % 2
            tk = None
            for k in range(NCH):
                tk = pe(lambda E, k=k, buf=buf, bank=bank: E.matmul(
                    pb[bank][:, 0:128], buf[:, 128 * k:128 * k + 128],
                    xh_fm[:, 128 * k:128 * k + 128],
                    start=(k == 0), stop=(k == NCH - 1)),
                    wait=[(wsem, tl), ("ve", t_xhfm)] + (bw(bank) if k == 0 else []))
            for k in range(NCH):
                tk = pe(lambda E, k=k, buf=buf, bank=bank: E.matmul(
                    pb[bank][:, 128:192], buf[:, 128 * k:128 * k + 128],
                    xg_fm[:, 64 * k:64 * k + 64],
                    start=(k == 0), stop=(k == NCH - 1)),
                    wait=[("ve", t_xgfm)])
            wcol_used[mi] = tk
            t_kp = ve(lambda E, m=m, bank=bank: E.tensor_copy(
                kprev[:, 128 * (m - 8):128 * (m - 7)], pb[bank][:, 0:128]),
                wait=[("pe", tk)])
            t_kg = ve(lambda E, m=m, bank=bank: E.tensor_copy(
                kglob[:, 64 * (m - 8):64 * (m - 7)], pb[bank][:, 128:192]),
                wait=[("pe", tk)])
            bankfree[bank] = ("ve", t_kg)
        wcol_prev_last2 = [wcol_used[6], wcol_used[7]]

        wrow_used = {}
        tkh = tkg2 = None
        for k in range(NCH):
            buf = wrow[k % 3]
            wsem = f"wr{k % 3}"
            wb = [("pe", wrow_used[k - 3])] if k >= 3 else []
            tl = dma(buf[:], wqkv[128 * k:128 * (k + 1), 2048:3072], wsem, wait=wb)
            for half in range(2):
                tkh = pe(lambda E, k=k, half=half, buf=buf: E.matmul(
                    pb[2 + half][:], xh_fm[:, 128 * k:128 * k + 128],
                    buf[:, 512 * half:512 * half + 512],
                    start=(k == 0), stop=(k == NCH - 1)),
                    wait=[(wsem, tl), ("ve", t_xhfm)] + (bw(2 + half) if k == 0 else []))
            for half in range(2):
                tkg2 = pe(lambda E, k=k, half=half, buf=buf: E.matmul(
                    pb[4 + half][0:64, :], xg_fm[:, 64 * k:64 * k + 64],
                    buf[:, 512 * half:512 * half + 512],
                    start=(k == 0), stop=(k == NCH - 1)),
                    wait=[("ve", t_xgfm)] + (bw(4 + half) if k == 0 else []))
            wrow_used[k] = tkg2
        wrow_prev_last3 = [wrow_used[5], wrow_used[6], wrow_used[7]]
        t_vp = t_vg = None
        for half in range(2):
            t_vp = ve(lambda E, half=half: E.tensor_copy(
                vprev[:, 512 * half:512 * half + 512], pb[2 + half][:]),
                wait=[("pe", tkh)])
            bankfree[2 + half] = ("ve", t_vp)
        for half in range(2):
            t_vg = ve(lambda E, half=half: E.tensor_copy(
                vglob[0:64, 512 * half:512 * half + 512], pb[4 + half][0:64, :]),
                wait=[("pe", tkg2)])
            bankfree[4 + half] = ("ve", t_vg)
        prev_tok = max(t_kp, t_vp)

        t_wfin = None
        for dst, asrc, bsrc in ((wtr, aer_in, ber_in), (wti, aei_in, bei_in)):
            for k in range(4):
                for c in range(4):
                    ia, ib = (2 * c) % 3, (2 * c + 1) % 3
                    ba = wrow[ia]
                    bb = wrow[ib]
                    wbv = [("ve", t_wfin)] if t_wfin else \
                          [("pe", wrow_used[NCH - 1])]
                    ta = dma(ba[:, 0:512], asrc[c, 128 * k:128 * k + 128, :],
                             f"wr{ia}", wait=wbv)
                    tb = dma(bb[:, 0:512], bsrc[c, 128 * k:128 * k + 128, :],
                             f"wr{ib}", wait=wbv)
                    if c == 0:
                        t_wfin = ve(lambda E, dst=dst, k=k, ba=ba, bb=bb:
                                    E.tensor_tensor(
                                        dst[:, 512 * k:512 * (k + 1)],
                                        ba[:, 0:512], bb[:, 0:512], ALU.mult),
                                    wait=[(f"wr{ia}", ta), (f"wr{ib}", tb)])
                    else:
                        t_wfin = ve(lambda E, ba=ba, bb=bb: E.tensor_tensor(
                            prodscr[:, 0:512], ba[:, 0:512], bb[:, 0:512], ALU.mult),
                            wait=[(f"wr{ia}", ta), (f"wr{ib}", tb)])
                        t_wfin = ve(lambda E, dst=dst, k=k: E.tensor_tensor(
                            dst[:, 512 * k:512 * (k + 1)],
                            dst[:, 512 * k:512 * (k + 1)], prodscr[:, 0:512], ALU.add))
        for k in range(4):
            t_wfin = ve(lambda E, k=k: E.tensor_scalar(
                wtn[:, 512 * k:512 * (k + 1)], wti[:, 512 * k:512 * (k + 1)],
                cpool[:, C_NEG1:C_NEG1 + 1], None, ALU.mult),
                wait=[("const", T_CONST)])
        wrow_wbuild_tok = t_wfin

        t_xload = {}
        for sub in range(NSUB):
            wpro = [("ve", t_xgfm)] if sub < 2 else []
            t_xload[(0, sub)] = dma(x_tm[sub][:],
                                    xm[HALO + 128 * sub:HALO + 128 * (sub + 1), :],
                                    f"x{sub}", wait=wpro)

        if dbg:
            store(dbg_t["dbg_wtr"], wtr[:], wait=[("ve", t_wfin)])
            store(dbg_t["dbg_wti"], wti[:], wait=[("ve", t_wfin)])
            store(dbg_t["dbg_kprev"], kprev[:], wait=[("ve", prev_tok)])
            store(dbg_t["dbg_kglob"], kglob[:], wait=[("ve", t_kg)])
            store(dbg_t["dbg_vglob"], vglob[0:64, :], wait=[("ve", t_vg)])
            store(dbg_t["dbg_vprev"], vprev[:], wait=[("ve", t_vp)])

        t_sqscr_rd = None
        t_attn_done_prev = None
        t_proj_done_prev = None
        t_ffn_done_prev = None
        t_resid = {}
        t_store = {}
        t_ln1nrm_all = {}
        t_ln2nrm_all = {}
        wout_prev_last3 = wrow_prev_last3

        for t in range(nt):
            xb = [x_tm[s] for s in range(NSUB)]

            w_ffnprev = [("pe", t_ffn_done_prev)] if t_ffn_done_prev else []
            t_xT = None
            for sub in range(NSUB):
                for g in range(2):
                    bank = 6 + g
                    tk = None
                    for c in range(4 * g, 4 * g + 4):
                        tk = pe(lambda E, c=c, bank=bank, x=xb[sub]: E.transpose(
                            pb[bank][:, 128 * (c % 4):128 * (c % 4) + 128],
                            x[:, 128 * c:128 * c + 128], ident[:]),
                            wait=[(f"x{sub}", t_xload[(t, sub)])] + bw(bank))
                    for c in range(4 * g, 4 * g + 4):
                        t_xT = ve(lambda E, c=c, sub=sub, bank=bank: E.tensor_copy(
                            xT[:, 512 * c + 128 * sub:512 * c + 128 * sub + 128],
                            pb[bank][:, 128 * (c % 4):128 * (c % 4) + 128]),
                            wait=[("pe", tk)] + w_ffnprev)
                    bankfree[bank] = ("ve", t_xT)

            w_attnprev = [("pe", t_attn_done_prev)] if t_attn_done_prev else []
            t_q_tok = {}
            t_k_tok = {}
            for m in range(16):
                buf = wcol[m % 2]
                wsem = f"wc{m % 2}"
                wb = [("pe", wcol_used[m - 2])] if m >= 2 else \
                     [("pe", wcol_prev_last2[m])]
                tl = dma(buf[:], wcol_src(m), wsem, wait=wb)
                bank = m % 2
                tk = None
                for k in range(NCH):
                    tk = pe(lambda E, k=k, buf=buf, bank=bank: E.matmul(
                        pb[bank][:], buf[:, 128 * k:128 * k + 128],
                        xT[:, 512 * k:512 * (k + 1)],
                        start=(k == 0), stop=(k == NCH - 1)),
                        wait=[(wsem, tl), ("ve", t_xT)] + (bw(bank) if k == 0 else []))
                wcol_used[m] = tk
                if m < 8:
                    tq = sc(lambda E, m=m, bank=bank: E.activation(
                        qfm[:, 512 * m:512 * (m + 1)], pb[bank][:],
                        AF.Copy, scale=0.125),
                        wait=[("pe", tk)] + w_attnprev)
                    t_q_tok[m] = tq
                    bankfree[bank] = ("sc", tq)
                else:
                    tkk = ve(lambda E, m=m, bank=bank: E.tensor_copy(
                        kcur[:, 512 * (m - 8):512 * (m - 7)], pb[bank][:]),
                        wait=[("pe", tk), ("ve", prev_tok)] + w_attnprev)
                    t_k_tok[m - 8] = tkk
                    bankfree[bank] = ("ve", tkk)
            wcol_prev_last2 = [wcol_used[14], wcol_used[15]]

            t_v_tok = {}
            for k in range(NCH):
                buf = wrow[k % 3]
                wsem = f"wr{k % 3}"
                if k >= 3:
                    wb = [("pe", wrow_used[k - 3])]
                elif t == 0:
                    wb = [("ve", wrow_wbuild_tok)]
                else:
                    wb = [("pe", wout_prev_last3[k])]
                tl = dma(buf[:], wqkv[128 * k:128 * (k + 1), 2048:3072], wsem, wait=wb)
                tk = None
                for s in range(NSUB):
                    for half in range(2):
                        bank = 2 * s + half
                        tk = pe(lambda E, k=k, s=s, half=half, buf=buf, bank=bank:
                                E.matmul(
                                    pb[bank][:],
                                    xT[:, 512 * k + 128 * s:512 * k + 128 * s + 128],
                                    buf[:, 512 * half:512 * half + 512],
                                    start=(k == 0), stop=(k == NCH - 1)),
                                wait=[(wsem, tl), ("ve", t_xT)] +
                                     (bw(bank) if k == 0 else []))
                wrow_used[k] = tk
            for s in range(NSUB):
                for half in range(2):
                    bank = 2 * s + half
                    tvv = ve(lambda E, s=s, half=half, bank=bank: E.tensor_copy(
                        vcur[s][:, 512 * half:512 * half + 512], pb[bank][:]),
                        wait=[("pe", wrow_used[NCH - 1]), ("ve", prev_tok)] + w_attnprev)
                    t_v_tok[s] = tvv
                    bankfree[bank] = ("ve", tvv)

            w_projprev = [("pe", t_proj_done_prev)] if t_proj_done_prev else []
            it = 0
            t_attn_done = None
            t_ao_tok = {}
            t_exp_par = [None, None]
            t_trans_par = [None, None]
            t_av_par = [None, None]
            for pair in range(8):
                av_bank = 4 + (pair % 2)
                for r in range(NSUB):
                    midx = min(4 * t + r, 2)
                    for hh in range(2):
                        h = 2 * pair + hh
                        i2 = it % 2
                        sb = it % 2
                        tb = 2 + (it % 2)
                        q_ap = qfm[64 * hh:64 * hh + 64,
                                   512 * pair + 128 * r:512 * pair + 128 * r + 128]
                        wsc = [("sc", t_q_tok[pair]), ("ve", t_k_tok[pair])] + bw(sb)
                        tk1 = pe(lambda E, q_ap=q_ap, hh=hh, pair=pair, sb=sb: E.matmul(
                            pb[sb][:, 0:64], q_ap,
                            kglob[64 * hh:64 * hh + 64, 64 * pair:64 * pair + 64],
                            start=True, stop=True), wait=wsc)
                        if r == 0:
                            tk1 = pe(lambda E, q_ap=q_ap, hh=hh, pair=pair, sb=sb:
                                     E.matmul(
                                         pb[sb][:, 64:192], q_ap,
                                         kprev[64 * hh:64 * hh + 64,
                                               128 * pair:128 * pair + 128],
                                         start=True, stop=True),
                                     wait=[("ve", prev_tok)])
                            tk1 = pe(lambda E, q_ap=q_ap, hh=hh, pair=pair, sb=sb:
                                     E.matmul(
                                         pb[sb][:, 192:320], q_ap,
                                         kcur[64 * hh:64 * hh + 64,
                                              512 * pair:512 * pair + 128],
                                         start=True, stop=True))
                        else:
                            tk1 = pe(lambda E, q_ap=q_ap, hh=hh, pair=pair, r=r, sb=sb:
                                     E.matmul(
                                         pb[sb][:, 64:320], q_ap,
                                         kcur[64 * hh:64 * hh + 64,
                                              512 * pair + 128 * (r - 1):
                                              512 * pair + 128 * (r + 1)],
                                         start=True, stop=True))
                        tmx = ve(lambda E, sb=sb, i2=i2: E.reduce_max(
                            smax[i2][:], pb[sb][:, 0:320], axis=AX.X, negate=True),
                            wait=[("pe", tk1)] +
                                 ([("sc", t_exp_par[i2])] if t_exp_par[i2] else []))
                        tex = sc(lambda E, sb=sb, i2=i2: E.activation(
                            pexp[i2][:], pb[sb][:, 0:320], AF.Exp,
                            bias=smax[i2][:], scale=1.0),
                            wait=[("ve", tmx)] +
                                 ([("pe", t_trans_par[i2])] if t_trans_par[i2] else []))
                        t_exp_par[i2] = tex
                        bankfree[sb] = ("sc", tex)
                        w_d1 = []
                        if dbg and t == 0 and it == dbg_it:
                            td = store(dbg_t["dbg_smax"], smax[i2][:],
                                       wait=[("ve", tmx)])
                            td = store(dbg_t["dbg_p1"], pexp[i2][:],
                                       wait=[("sc", tex)])
                            w_d1 = [("dbgst", td)]
                        tms = ve(lambda E, i2=i2, midx=midx: E.scalar_tensor_tensor(
                            pexp[i2][:], pexp[i2][:], cpool[:, C_ONE:C_ONE + 1],
                            masks[:, 320 * midx:320 * (midx + 1)],
                            ALU.mult, ALU.mult, accum_out=ssum[i2][:]),
                            wait=[("sc", tex), ("const", T_CONST)] + w_d1)
                        w_d2 = []
                        if dbg and t == 0 and it == dbg_it:
                            td = store(dbg_t["dbg_p2"], pexp[i2][:],
                                       wait=[("ve", tms)])
                            td = store(dbg_t["dbg_ssum"], ssum[i2][:],
                                       wait=[("ve", tms)])
                            w_d2 = [("dbgst", td)]
                        trec = ve(lambda E, i2=i2: E.reciprocal(
                            srinv[i2][:], ssum[i2][:]),
                            wait=[("ve", tms)] + w_d2)
                        tnm = ve(lambda E, i2=i2: E.tensor_scalar(
                            pexp[i2][:], pexp[i2][:], srinv[i2][:], None, ALU.mult),
                            wait=[("ve", trec)])
                        w_d3 = []
                        if dbg and t == 0 and it == dbg_it:
                            td = store(dbg_t["dbg_p3"], pexp[i2][:],
                                       wait=[("ve", tnm)])
                            w_d3 = [("dbgst", td)]
                        tk2 = pe(lambda E, i2=i2, tb=tb: E.transpose(
                            pb[tb][0:64, 0:128], pexp[i2][:, 0:64], ident[:]),
                            wait=[("ve", tnm)] + bw(tb) + w_d3)
                        tk2 = pe(lambda E, i2=i2, tb=tb: E.transpose(
                            pb[tb][:, 128:256], pexp[i2][:, 64:192], ident[:]))
                        tk2 = pe(lambda E, i2=i2, tb=tb: E.transpose(
                            pb[tb][:, 256:384], pexp[i2][:, 192:320], ident[:]))
                        t_trans_par[i2] = tk2
                        tpe = sc(lambda E, i2=i2, tb=tb: E.copy(
                            pT_sb[i2][:], pb[tb][:, 0:384]),
                            wait=[("pe", tk2)] +
                                 ([("pe", t_av_par[i2])] if t_av_par[i2] else []))
                        bankfree[tb] = ("sc", tpe)
                        tp_av = (0, 64) if hh else None
                        if dbg and t == 0 and it == dbg_it:
                            td = store(dbg_t["dbg_pt"], pT_sb[i2][:],
                                       wait=[("sc", tpe)])
                            p.lastw["tensor"].pop("dbgst", None)
                        wav = ([("dbgst", td)] if (dbg and t == 0 and it == dbg_it) else []) + \
                              [("sc", tpe), ("ve", t_v_tok[r]), ("ve", t_vg),
                               ("ve", t_vp)] + \
                              ([("ve", t_v_tok[r - 1])] if r >= 1 else []) + \
                              (bw(av_bank) if (r == 0 and hh == 0) else [])
                        vband = vprev if r == 0 else vcur[r - 1]
                        tk3 = pe(lambda E, h=h, hh=hh, r=r, i2=i2, ab=av_bank,
                                 tp=tp_av: E.matmul(
                                     pb[ab][64 * hh:64 * hh + 64,
                                            128 * r:128 * r + 128],
                                     vglob[0:64, 64 * h:64 * h + 64],
                                     pT_sb[i2][0:64, 0:128],
                                     start=True, stop=False, tile_position=tp),
                                 wait=wav)
                        tk3 = pe(lambda E, h=h, hh=hh, r=r, i2=i2, ab=av_bank,
                                 vb=vband, tp=tp_av: E.matmul(
                                     pb[ab][64 * hh:64 * hh + 64,
                                            128 * r:128 * r + 128],
                                     vb[:, 64 * h:64 * h + 64],
                                     pT_sb[i2][:, 128:256],
                                     start=False, stop=False, tile_position=tp))
                        tk3 = pe(lambda E, h=h, hh=hh, r=r, i2=i2, ab=av_bank,
                                 tp=tp_av: E.matmul(
                                     pb[ab][64 * hh:64 * hh + 64,
                                            128 * r:128 * r + 128],
                                     vcur[r][:, 64 * h:64 * h + 64],
                                     pT_sb[i2][:, 256:384],
                                     start=False, stop=True, tile_position=tp))
                        t_av_par[i2] = tk3
                        it += 1
                t_attn_done = tk3
                tao = ve(lambda E, pair=pair, ab=av_bank: E.tensor_copy(
                    aoT[:, 512 * pair:512 * (pair + 1)], pb[ab][:]),
                    wait=[("pe", tk3)] + w_projprev)
                t_ao_tok[pair] = tao
                bankfree[av_bank] = ("ve", tao)

            if t + 1 < nt:
                tpc = None
                for c in range(NCH):
                    tpc = ve(lambda E, c=c: E.tensor_copy(
                        kprev[:, 128 * c:128 * c + 128],
                        kcur[:, 512 * c + 384:512 * c + 512]),
                        wait=[("pe", t_attn_done)])
                tpv = ve(lambda E: E.tensor_copy(vprev[:], vcur[3][:]),
                         wait=[("pe", t_attn_done)])
                prev_tok = tpv

            t_wout_load = {}
            t_wout_used = {}
            t_proj_bank = {}
            for k in range(NCH):
                buf = wrow[k % 3]
                wsem = f"wr{k % 3}"
                if k >= 3:
                    wb = [("pe", t_wout_used[k - 3])]
                else:
                    wb = [("pe", wrow_used[{0: 6, 1: 7, 2: 5}[k]])]
                tl = dma(buf[:], wout[128 * k:128 * (k + 1), :], wsem, wait=wb)
                t_wout_load[k] = tl
                tk = None
                for r in range(NSUB):
                    for half in range(2):
                        bank = 2 * r + half
                        tk = pe(lambda E, k=k, r=r, half=half, buf=buf, bank=bank:
                                E.matmul(
                                    pb[bank][:],
                                    aoT[:, 512 * k + 128 * r:512 * k + 128 * r + 128],
                                    buf[:, 512 * half:512 * half + 512],
                                    start=(k == 0), stop=(k == NCH - 1)),
                                wait=[(wsem, tl), ("ve", t_ao_tok[k])] +
                                     (bw(bank) if k == 0 else []))
                        if k == NCH - 1:
                            t_proj_bank[(r, half)] = tk
                t_wout_used[k] = tk
            wout_prev_last3 = [t_wout_used[6], t_wout_used[7], t_wout_used[5]]
            t_proj_done = t_wout_used[NCH - 1]
            t_ln1pre = {}
            for r in range(NSUB):
                tres = None
                for half in range(2):
                    bank = 2 * r + half
                    tres = ve(lambda E, r=r, half=half, bank=bank, x=xb[r]:
                              E.scalar_tensor_tensor(
                                  ln1tm[r][:, 512 * half:512 * half + 512],
                                  pb[bank][:], cpool[:, C_ONE:C_ONE + 1],
                                  x[:, 512 * half:512 * half + 512],
                                  ALU.mult, ALU.add),
                              wait=[("pe", t_proj_bank[(r, half)])])
                    bankfree[bank] = ("ve", tres)
                t_ln1pre[r] = tres
            t_resid[t] = t_ln1pre[NSUB - 1]
            if t + 1 < nt:
                for sub in range(NSUB):
                    t_xload[(t + 1, sub)] = dma(
                        x_tm[sub][:],
                        xm[HALO + 512 * (t + 1) + 128 * sub:
                           HALO + 512 * (t + 1) + 128 * (sub + 1), :],
                        f"x{sub}", wait=[("ve", t_ln1pre[sub])])

            t_ln1fm = None
            t_ln1nrm = {}
            for r in range(NSUB):
                st0 = stat[r % 2]
                if (r - 2) in t_ln1nrm:
                    wpar = [("ve", t_ln1nrm[r - 2])]
                elif t > 0:
                    wpar = [("ve", t_ln1nrm_all[t - 1][r + 2])]
                else:
                    wpar = []
                si_ = r % 2
                tsqr = sc(lambda E, r=r: E.activation(
                    sqscr[:], ln1tm[r][:], AF.Square),
                    wait=[("ve", t_ln1pre[r])] + wpar +
                         ([("ve", t_sqscr_rd)] if t_sqscr_rd else []))
                tsq = ve(lambda E, si_=si_: E.reduce_sum(
                    s_sq[si_][:], sqscr[:], axis=AX.X),
                    wait=[("sc", tsqr)])
                t_sqscr_rd = tsq
                tred = ve(lambda E, si_=si_, r=r: E.reduce_sum(
                    s_sum[si_][:], ln1tm[r][:], axis=AX.X))
                tt = ve(lambda E, si_=si_: E.tensor_scalar(
                    s_mean[si_][:], s_sum[si_][:], cpool[:, C_INV1024:C_INV1024 + 1],
                    None, ALU.mult), wait=[("const", T_CONST), ("ve", tred)])
                tt = ve(lambda E, si_=si_: E.tensor_scalar(
                    s_msq[si_][:], s_sq[si_][:], cpool[:, C_INV1024:C_INV1024 + 1],
                    None, ALU.mult), wait=[("ve", tsq), ("ve", tt)])
                tt = ve(lambda E, si_=si_: E.tensor_tensor(
                    s_var[si_][:], s_mean[si_][:], s_mean[si_][:], ALU.mult),
                    wait=[("ve", tt)])
                tt = ve(lambda E, si_=si_: E.tensor_tensor(
                    s_var[si_][:], s_msq[si_][:], s_var[si_][:], ALU.subtract),
                    wait=[("ve", tt)])
                tvar = ve(lambda E, si_=si_: E.tensor_scalar(
                    s_var[si_][:], s_var[si_][:], cpool[:, C_EPS:C_EPS + 1],
                    None, ALU.add), wait=[("ve", tt)])
                tsd = sc(lambda E, si_=si_: E.activation(
                    s_sd[si_][:], s_var[si_][:], AF.Sqrt),
                    wait=[("ve", tvar)])
                trc = ve(lambda E, si_=si_: E.reciprocal(
                    s_rs[si_][:], s_sd[si_][:]), wait=[("sc", tsd)])
                tsh = ve(lambda E, si_=si_: E.scalar_tensor_tensor(
                    s_sh[si_][:], s_mean[si_][:], cpool[:, C_NEG1:C_NEG1 + 1],
                    s_rs[si_][:], ALU.mult, ALU.mult), wait=[("ve", trc)])
                tnm = ve(lambda E, si_=si_, r=r: E.tensor_scalar(
                    ln1tm[r][:], ln1tm[r][:], s_rs[si_][:], s_sh[si_][:],
                    ALU.mult, ALU.add), wait=[("ve", tsh)])
                t_ln1nrm[r] = tnm
                for g in range(2):
                    bank = 2 + g
                    tk = None
                    for c in range(4 * g, 4 * g + 4):
                        tk = pe(lambda E, c=c, r=r, bank=bank: E.transpose(
                            pb[bank][:, 128 * (c % 4):128 * (c % 4) + 128],
                            ln1tm[r][:, 128 * c:128 * c + 128], ident[:]),
                            wait=[("ve", tnm)] + bw(bank))
                    for c in range(4 * g, 4 * g + 4):
                        t_ln1fm = ve(lambda E, c=c, r=r, bank=bank: E.tensor_copy(
                            xT[:, 512 * c + 128 * r:512 * c + 128 * r + 128],
                            pb[bank][:, 128 * (c % 4):128 * (c % 4) + 128]),
                            wait=[("pe", tk)])
                    bankfree[bank] = ("ve", t_ln1fm)
            t_ln1nrm_all[t] = t_ln1nrm

            t_ffn_done = None
            t_ln2nrm = {}
            for r in range(NSUB):
                banks = (4, 5) if r % 2 == 0 else (6, 7)
                tk = None
                for c in range(4):
                    lhs = xT[:, 512 * c + 128 * r:512 * c + 128 * r + 128]
                    tk = pe(lambda E, lhs=lhs, c=c, bank=banks[0]: E.matmul(
                        pb[bank][:], lhs, wtr[:, 512 * c:512 * (c + 1)],
                        start=(c == 0), stop=False),
                        wait=[("ve", t_ln1fm), ("ve", t_wfin)] +
                             (bw(banks[0]) if c == 0 else []))
                    tk = pe(lambda E, lhs=lhs, c=c, bank=banks[1]: E.matmul(
                        pb[bank][:], lhs, wti[:, 512 * c:512 * (c + 1)],
                        start=(c == 0), stop=False),
                        wait=(bw(banks[1]) if c == 0 else []))
                for c in range(4):
                    lhs = xT[:, 512 * (4 + c) + 128 * r:512 * (4 + c) + 128 * r + 128]
                    tk = pe(lambda E, lhs=lhs, c=c, bank=banks[0]: E.matmul(
                        pb[bank][:], lhs, wtn[:, 512 * c:512 * (c + 1)],
                        start=False, stop=False))
                    tk = pe(lambda E, lhs=lhs, c=c, bank=banks[1]: E.matmul(
                        pb[bank][:], lhs, wtr[:, 512 * c:512 * (c + 1)],
                        start=False, stop=False))
                tk = pe(lambda E, bank=banks[0]: E.matmul(
                    pb[bank][:], ones1[:], fbias[:, 0:512],
                    start=False, stop=True), wait=[("const", T_CONST)])
                tk = pe(lambda E, bank=banks[1]: E.matmul(
                    pb[bank][:], ones1[:], fbias[:, 512:1024],
                    start=False, stop=True))
                t_ffn_done = tk
                ob = otm[r % 2]
                stsem = f"st{r % 2}"
                if (t, r - 2) in t_store:
                    wst = [(stsem, t_store[(t, r - 2)])]
                elif (t - 1, r + 2) in t_store:
                    wst = [(stsem, t_store[(t - 1, r + 2)])]
                else:
                    wst = []
                tpre = None
                for half in range(2):
                    bank = banks[half]
                    tpre = ve(lambda E, r=r, half=half, bank=bank, ob=ob:
                              E.scalar_tensor_tensor(
                                  ob[:, 512 * half:512 * half + 512],
                                  pb[bank][:], cpool[:, C_ONE:C_ONE + 1],
                                  ln1tm[r][:, 512 * half:512 * half + 512],
                                  ALU.mult, ALU.add),
                              wait=[("pe", tk)] + wst)
                    bankfree[bank] = ("ve", tpre)
                st1 = stat[2 + r % 2]
                wpar = [("ve", t_ln2nrm[r - 2])] if (r - 2) in t_ln2nrm else \
                       ([("ve", t_ln2nrm_all[t - 1][r + 2])] if t > 0 else [])
                si_ = 2 + r % 2
                tsqr = sc(lambda E, ob=ob: E.activation(
                    sqscr[:], ob[:], AF.Square),
                    wait=[("ve", tpre)] + wpar +
                         ([("ve", t_sqscr_rd)] if t_sqscr_rd else []))
                tsq = ve(lambda E, si_=si_: E.reduce_sum(
                    s_sq[si_][:], sqscr[:], axis=AX.X),
                    wait=[("sc", tsqr)])
                t_sqscr_rd = tsq
                tred = ve(lambda E, si_=si_, ob=ob: E.reduce_sum(
                    s_sum[si_][:], ob[:], axis=AX.X))
                tt = ve(lambda E, si_=si_: E.tensor_scalar(
                    s_mean[si_][:], s_sum[si_][:], cpool[:, C_INV1024:C_INV1024 + 1],
                    None, ALU.mult), wait=[("const", T_CONST), ("ve", tred)])
                tt = ve(lambda E, si_=si_: E.tensor_scalar(
                    s_msq[si_][:], s_sq[si_][:], cpool[:, C_INV1024:C_INV1024 + 1],
                    None, ALU.mult), wait=[("ve", tsq), ("ve", tt)])
                tt = ve(lambda E, si_=si_: E.tensor_tensor(
                    s_var[si_][:], s_mean[si_][:], s_mean[si_][:], ALU.mult),
                    wait=[("ve", tt)])
                tt = ve(lambda E, si_=si_: E.tensor_tensor(
                    s_var[si_][:], s_msq[si_][:], s_var[si_][:], ALU.subtract),
                    wait=[("ve", tt)])
                tvar = ve(lambda E, si_=si_: E.tensor_scalar(
                    s_var[si_][:], s_var[si_][:], cpool[:, C_EPS:C_EPS + 1],
                    None, ALU.add), wait=[("ve", tt)])
                tsd = sc(lambda E, si_=si_: E.activation(
                    s_sd[si_][:], s_var[si_][:], AF.Sqrt),
                    wait=[("ve", tvar)])
                trc = ve(lambda E, si_=si_: E.reciprocal(
                    s_rs[si_][:], s_sd[si_][:]), wait=[("sc", tsd)])
                tsh = ve(lambda E, si_=si_: E.scalar_tensor_tensor(
                    s_sh[si_][:], s_mean[si_][:], cpool[:, C_NEG1:C_NEG1 + 1],
                    s_rs[si_][:], ALU.mult, ALU.mult), wait=[("ve", trc)])
                tnm = ve(lambda E, si_=si_, ob=ob: E.tensor_scalar(
                    ob[:], ob[:], s_rs[si_][:], s_sh[si_][:],
                    ALU.mult, ALU.add), wait=[("ve", tsh)])
                t_ln2nrm[r] = tnm
                t_store[(t, r)] = store(
                    y[512 * t + 128 * r:512 * t + 128 * (r + 1), :], ob[:],
                    sem=stsem, wait=[("ve", tnm)])
            t_ln2nrm_all[t] = t_ln2nrm

            if dbg and t == 0:
                for si in range(4):
                    store(dbg_t["dbg_stat"][si], stat[si][:],
                          wait=[("ve", t_ln2nrm[NSUB - 1])])
                store(dbg_t["dbg_xt"], xT[:], wait=[("ve", t_ln1fm)])
                store(dbg_t["dbg_qfm"], qfm[:], wait=[("sc", t_q_tok[7])])
                store(dbg_t["dbg_kcur"], kcur[:], wait=[("ve", t_k_tok[7])])
                for s in range(NSUB):
                    store(dbg_t["dbg_vcur"][s], vcur[s][:], wait=[("ve", t_v_tok[s])])
                store(dbg_t["dbg_aot"], aoT[:], wait=[("ve", t_ao_tok[7])])
                for r in range(NSUB):
                    store(dbg_t["dbg_ln1"][r], ln1tm[r][:],
                          wait=[("ve", t_ln1nrm[r])])

            t_attn_done_prev = t_attn_done
            t_proj_done_prev = t_proj_done
            t_ffn_done_prev = t_ffn_done

        p.emit("gpsimd", None,
               wait=[(s, p.ctr[s]) for s in ("st0", "st1", "dbgst")
                     if p.ctr[s] > 0])

        globals()["_last_plan"] = dict(p.ctr)
        with nc.Block() as block:
            def mk(eng_name):
                def body(E):
                    for waits, fn, inc, amt in p.ops[eng_name]:
                        for s, v in waits:
                            E.wait_ge(sems[s], v)
                        if fn is not None:
                            inst = fn(E)
                            if inc:
                                inst.then_inc(sems[inc], amt)
                return body
            block.sync(mk("sync"))
            block.tensor(mk("tensor"))
            block.vector(mk("vector"))
            block.scalar(mk("scalar"))
            block.gpsimd(mk("gpsimd"))

    return nc



def make_core_inputs(x, w_qkv, w_out, G1_R, G2_R, G1_I, G2_I, b_r, b_i):
    x = np.ascontiguousarray(np.asarray(x, dtype=np.float32))
    w_qkv = np.ascontiguousarray(np.asarray(w_qkv, dtype=np.float32))
    w_out = np.ascontiguousarray(np.asarray(w_out, dtype=np.float32))
    r_idx = np.arange(128)[:, None]
    c_idx = np.arange(128)[None, :]
    low = (c_idx <= r_idx).astype(np.float32)
    up = (c_idx >= r_idx).astype(np.float32)
    up64 = ((c_idx >= r_idx) | (c_idx < 64)).astype(np.float32)
    g0 = np.zeros((128, 64), np.float32)
    g1 = np.ones((128, 64), np.float32)
    z128 = np.zeros((128, 128), np.float32)
    masks_h0 = np.ascontiguousarray(np.stack([
        np.concatenate([g0, z128, low], axis=1),
        np.concatenate([g0, up64, low], axis=1),
        np.concatenate([g1, up, low], axis=1)]))
    m2 = np.concatenate([g1, up, low], axis=1)
    masks_h1 = np.ascontiguousarray(np.stack([m2, m2, m2]))
    ident = np.eye(128, dtype=np.float32)
    cpool = np.zeros((128, 8), np.float32)
    cpool[:, C_INV1024] = 1.0 / 1024.0
    cpool[:, C_EPS] = 1e-5
    cpool[:, C_NEG1] = -1.0
    cpool[:, C_ONE] = 1.0

    oi = np.arange(512)
    aexp = {}
    bexp = {}
    for tag, G1, G2 in (("r", G1_R, G2_R), ("i", G1_I, G2_I)):
        G1 = np.asarray(G1, np.float32)
        G2 = np.asarray(G2, np.float32)
        a = np.empty((4, 512, 512), np.float32)
        b = np.empty((4, 512, 512), np.float32)
        for c in range(4):
            A = G1[0, :, c, :]
            Bm = G2[c, :, 0, :]
            a[c] = A[oi // 16, :][:, oi // 16].T
            b[c] = Bm[oi % 16, :][:, oi % 16].T
        aexp[tag] = np.ascontiguousarray(a)
        bexp[tag] = np.ascontiguousarray(b)
    fbias = np.ascontiguousarray(
        np.concatenate([np.asarray(b_r, np.float32),
                        np.asarray(b_i, np.float32)])[None, :])

    in_maps = []
    for core in range(8):
        b = core // 2
        half = core % 2
        if half == 0:
            x_main = np.concatenate([np.zeros((HALO, D), np.float32),
                                     x[b, 0:OWN]], axis=0)
        else:
            x_main = x[b, OWN - HALO:SEQ]
        in_maps.append({
            "x_main": np.ascontiguousarray(x_main),
            "x_glob": np.ascontiguousarray(x[b, 0:GLOB]),
            "w_qkv": w_qkv,
            "w_out": w_out,
            "masks": masks_h0 if half == 0 else masks_h1,
            "ident": ident,
            "cpool": cpool,
            "aexp_r": aexp["r"], "bexp_r": bexp["r"],
            "aexp_i": aexp["i"], "bexp_i": bexp["i"],
            "fbias": fbias,
        })
    return in_maps


def kernel(L0_attn_x, L0_attn_w_qkv, L0_attn_w_out, L0_ffn_G1_R, L0_ffn_G2_R,
           L0_ffn_G1_I, L0_ffn_G2_I, L0_ffn_bias_real, L0_ffn_bias_imag):
    in_maps = make_core_inputs(L0_attn_x, L0_attn_w_qkv, L0_attn_w_out,
                               L0_ffn_G1_R, L0_ffn_G2_R, L0_ffn_G1_I,
                               L0_ffn_G2_I, L0_ffn_bias_real, L0_ffn_bias_imag)
    nc = build_program(dbg=False, nt=NT)
    res = run_bass_kernel_spmd(nc, in_maps, core_ids=list(range(8)))
    out = np.empty((NB, SEQ, D), dtype=np.float32)
    for core in range(8):
        b, half = core // 2, core % 2
        out[b, OWN * half:OWN * (half + 1)] = res.results[core]["y"]
    return out


# version 02
# speedup vs baseline: 1.1713x; 1.1713x over previous
import numpy as np
from contextlib import ExitStack
import concourse.bass as bass
import concourse.mybir as mybir
from concourse.bass_utils import run_bass_kernel_spmd

F32 = mybir.dt.float32
AF = mybir.ActivationFunctionType
ALU = mybir.AluOpType
AX = mybir.AxisListType

D = 1024
SEQ = 4096
NB = 4
NSUB = 4
NT = 4
OWN = 2048
HALO = 128
GLOB = 64
NCH = 8
KW = 704

C_INV1024 = 0
C_EPS = 1
C_NEG1 = 2
C_ONE = 3

ENGS = ("sync", "tensor", "vector", "scalar", "gpsimd")

DMA_SEMS = ("const", "x0", "x1", "x2", "x3", "wc0", "wc1",
            "wr0", "wr1", "wr2", "wbA", "wbB", "st0", "st1", "dbgst")


class Plan:
    def __init__(self):
        self.ops = {e: [] for e in ENGS}
        self.ctr = {n: 0 for n in ("pe", "ve", "sc") + DMA_SEMS}
        self.lastw = {e: {} for e in ENGS}

    def emit(self, eng, fn, wait=(), inc=None):
        w = []
        for sem, val in wait:
            if not val:
                continue
            if val > self.lastw[eng].get(sem, 0):
                w.append((sem, val))
                self.lastw[eng][sem] = val
        amt = 0
        if inc:
            amt = 16 if inc in DMA_SEMS else 1
            self.ctr[inc] += amt
        self.ops[eng].append((tuple(w), fn, inc, amt))
        return self.ctr[inc] if inc else None


def build_program(dbg=False, nt=NT):
    nc = bass.Bass("TRN2", target_bir_lowering=False)

    xm = nc.dram_tensor("x_main", [HALO + OWN, D], F32, kind="ExternalInput").ap()
    xg_in = nc.dram_tensor("x_glob", [GLOB, D], F32, kind="ExternalInput").ap()
    wqkv = nc.dram_tensor("w_qkv", [D, 3 * D], F32, kind="ExternalInput").ap()
    wout = nc.dram_tensor("w_out", [D, D], F32, kind="ExternalInput").ap()
    masks_in = nc.dram_tensor("masks", [3, 128, 320], F32, kind="ExternalInput").ap()
    ident_in = nc.dram_tensor("ident", [128, 128], F32, kind="ExternalInput").ap()
    cpool_in = nc.dram_tensor("cpool", [128, 8], F32, kind="ExternalInput").ap()
    aer_in = nc.dram_tensor("aexp_r", [4, 512, 512], F32, kind="ExternalInput").ap()
    ber_in = nc.dram_tensor("bexp_r", [4, 512, 512], F32, kind="ExternalInput").ap()
    aei_in = nc.dram_tensor("aexp_i", [4, 512, 512], F32, kind="ExternalInput").ap()
    bei_in = nc.dram_tensor("bexp_i", [4, 512, 512], F32, kind="ExternalInput").ap()
    bias_in = nc.dram_tensor("fbias", [1, 1024], F32, kind="ExternalInput").ap()
    y = nc.dram_tensor("y", [OWN, D], F32, kind="ExternalOutput").ap()
    dbg_t = {}
    if dbg:
        for name, shape in (("dbg_qfm", [128, 4096]), ("dbg_kwin", [128, 8 * KW]),
                            ("dbg_vcur", [4, 128, 1024]),
                            ("dbg_vglob", [64, 1024]), ("dbg_vprev", [128, 1024]),
                            ("dbg_aot", [128, 4096]), ("dbg_ln1", [4, 128, 1024]),
                            ("dbg_wtr", [128, 2048]), ("dbg_wti", [128, 2048]),
                            ("dbg_xt", [128, 4096])):
            dbg_t[name] = nc.dram_tensor(name, shape, F32, kind="ExternalOutput").ap()

    p = Plan()

    with ExitStack() as ctx:
        e = ctx.enter_context
        ident = e(nc.sbuf_tensor("ident_sb", [128, 128], F32))
        cpool = e(nc.sbuf_tensor("cpool_sb", [128, 8], F32))
        masks = e(nc.sbuf_tensor("masks_sb", [128, 3 * 320], F32))
        x_tm = [e(nc.sbuf_tensor(f"x_tm{i}", [128, D], F32)) for i in range(4)]
        xh_tm = x_tm[0]
        xg_tm = x_tm[1]
        xh_fm = e(nc.sbuf_tensor("xh_fm", [128, 1024], F32))
        xg_fm = e(nc.sbuf_tensor("xg_fm", [128, 512], F32))
        xT = e(nc.sbuf_tensor("xT", [128, 4096], F32))
        qfm = e(nc.sbuf_tensor("qfm", [128, 4096], F32))
        kwin = e(nc.sbuf_tensor("kwin", [128, 8 * KW], F32))
        vcur = [e(nc.sbuf_tensor(f"vcur{s}", [128, D], F32)) for s in range(4)]
        vprev = e(nc.sbuf_tensor("vprev", [128, D], F32))
        vglob = e(nc.sbuf_tensor("vglob", [64, D], F32))
        aoT = e(nc.sbuf_tensor("aoT", [128, 4096], F32))
        ln1tm = [e(nc.sbuf_tensor(f"ln1tm{s}", [128, D], F32)) for s in range(4)]
        otm = [e(nc.sbuf_tensor(f"otm{i}", [128, D], F32)) for i in range(2)]
        sqscr = xh_fm
        pexp = [e(nc.sbuf_tensor(f"pexp{i}", [128, 320], F32)) for i in range(3)]
        pT_sb = [e(nc.sbuf_tensor(f"pT{i}", [128, 384], F32)) for i in range(2)]
        wcol = [e(nc.sbuf_tensor(f"wcol{i}", [128, 1024], F32)) for i in range(2)]
        wrow = [e(nc.sbuf_tensor(f"wrow{i}", [128, 1024], F32)) for i in range(3)]
        wbuf = [e(nc.sbuf_tensor(f"wbuf{i}", [128, 512], F32)) for i in range(2)]
        wtr = e(nc.sbuf_tensor("wtr", [128, 2048], F32))
        wti = e(nc.sbuf_tensor("wti", [128, 2048], F32))
        wtn = e(nc.sbuf_tensor("wtn", [128, 2048], F32))
        fbias = e(nc.sbuf_tensor("fbias_sb", [1, 1024], F32))
        ones1 = e(nc.sbuf_tensor("ones1", [1, 128], F32))
        smax = [e(nc.sbuf_tensor(f"smax{i}", [128, 1], F32)) for i in range(3)]
        ssum = [e(nc.sbuf_tensor(f"ssum{i}", [128, 1], F32)) for i in range(3)]
        srinv = [e(nc.sbuf_tensor(f"srinv{i}", [128, 1], F32)) for i in range(3)]
        s_sum = [e(nc.sbuf_tensor(f"s_sum{i}", [128, 1], F32)) for i in range(4)]
        s_sq = [e(nc.sbuf_tensor(f"s_sq{i}", [128, 1], F32)) for i in range(4)]
        s_mean = [e(nc.sbuf_tensor(f"s_mean{i}", [128, 1], F32)) for i in range(4)]
        s_msq = [e(nc.sbuf_tensor(f"s_msq{i}", [128, 1], F32)) for i in range(4)]
        s_var = [e(nc.sbuf_tensor(f"s_var{i}", [128, 1], F32)) for i in range(4)]
        s_sd = [e(nc.sbuf_tensor(f"s_sd{i}", [128, 1], F32)) for i in range(4)]
        s_rs = [e(nc.sbuf_tensor(f"s_rs{i}", [128, 1], F32)) for i in range(4)]
        s_sh = [e(nc.sbuf_tensor(f"s_sh{i}", [128, 1], F32)) for i in range(4)]
        prodscr = vcur[0]

        pb = [e(nc.psum_tensor(f"pb{i}", [128, 512], F32)) for i in range(8)]
        bankfree = {i: None for i in range(8)}

        sems = {n: e(nc.semaphore(n)) for n in ("pe", "ve", "sc") + DMA_SEMS}

        def dma(dst, src, sem, wait=()):
            return p.emit("sync", lambda E, d=dst, s=src: E.dma_start(out=d, in_=s),
                          wait=wait, inc=sem)

        def gdma(dst, src, sem, wait=()):
            return p.emit("gpsimd", lambda E, d=dst, s=src: E.dma_start(out=d, in_=s),
                          wait=wait, inc=sem)

        def store(dst, src, sem="dbgst", wait=()):
            return p.emit("gpsimd", lambda E, d=dst, s=src: E.dma_start(out=d, in_=s),
                          wait=wait, inc=sem)

        def pe(fn, wait=()):
            return p.emit("tensor", fn, wait=wait, inc="pe")

        def ve(fn, wait=()):
            return p.emit("vector", fn, wait=wait, inc="ve")

        def sc(fn, wait=()):
            return p.emit("scalar", fn, wait=wait, inc="sc")

        def bw(i):
            return [bankfree[i]] if bankfree[i] else []

        dma(ident[:], ident_in, "const")
        dma(cpool[:], cpool_in, "const")
        for v in range(3):
            dma(masks[:, 320 * v:320 * (v + 1)], masks_in[v], "const")
        dma(xh_tm[:], xm[0:HALO, :], "const")
        dma(xg_tm[0:64, :], xg_in, "const")
        dma(fbias[:], bias_in, "const")
        T_CONST = p.ctr["const"]
        ve(lambda E: E.memset(ones1[:], 1.0))

        def wcol_src(m):
            return bass.AP(wqkv.tensor, 128 * m,
                           [[3072, 128], [128 * 3072, 8], [1, 128]])

        def fm_transpose_pro(src_tm, dst, n_tok, wait_extra):
            tv = None
            for g in range(2):
                bank = 6 + g
                tk = None
                for c in range(4 * g, 4 * g + 4):
                    tk = pe(lambda E, c=c, bank=bank: E.transpose(
                        pb[bank][:, n_tok * (c % 4):n_tok * (c % 4) + n_tok],
                        src_tm[0:n_tok, 128 * c:128 * c + 128],
                        ident[0:n_tok, 0:n_tok]),
                        wait=wait_extra + bw(bank))
                tv = ve(lambda E, g=g, bank=bank: E.tensor_copy(
                    dst[:, 4 * n_tok * g:4 * n_tok * (g + 1)],
                    pb[bank][:, 0:4 * n_tok]),
                    wait=[("pe", tk)])
                bankfree[bank] = ("ve", tv)
            return tv

        t_xhfm = fm_transpose_pro(xh_tm, xh_fm, 128, [("const", T_CONST)])
        t_xgfm = fm_transpose_pro(xg_tm, xg_fm, 64, [("const", T_CONST)])

        wcol_used = {}
        t_kp = None
        for mi, m in enumerate(range(8, 16)):
            buf = wcol[mi % 2]
            wsem = f"wc{mi % 2}"
            wb = [("pe", wcol_used[mi - 2])] if mi >= 2 else []
            tl = dma(buf[:], wcol_src(m), wsem, wait=wb)
            bank = mi % 2
            tk = None
            for k in range(NCH):
                tk = pe(lambda E, k=k, buf=buf, bank=bank: E.matmul(
                    pb[bank][:, 0:128], buf[:, 128 * k:128 * k + 128],
                    xh_fm[:, 128 * k:128 * k + 128],
                    start=(k == 0), stop=(k == NCH - 1)),
                    wait=[(wsem, tl), ("ve", t_xhfm)] + (bw(bank) if k == 0 else []))
            for k in range(NCH):
                tk = pe(lambda E, k=k, buf=buf, bank=bank: E.matmul(
                    pb[bank][:, 128:192], buf[:, 128 * k:128 * k + 128],
                    xg_fm[:, 64 * k:64 * k + 64],
                    start=(k == 0), stop=(k == NCH - 1)),
                    wait=[("ve", t_xgfm)])
            wcol_used[mi] = tk
            t_kp = ve(lambda E, m=m, bank=bank: E.tensor_copy(
                kwin[:, KW * (m - 8) + 64:KW * (m - 8) + 192], pb[bank][:, 0:128]),
                wait=[("pe", tk)])
            t_kp = ve(lambda E, m=m, bank=bank: E.tensor_copy(
                kwin[:, KW * (m - 8):KW * (m - 8) + 64], pb[bank][:, 128:192]),
                wait=[("pe", tk)])
            bankfree[bank] = ("ve", t_kp)
        wcol_prev_last2 = [wcol_used[6], wcol_used[7]]

        wrow_used = {}
        tkh = tkg2 = None
        for k in range(NCH):
            buf = wrow[k % 3]
            wsem = f"wr{k % 3}"
            wb = [("pe", wrow_used[k - 3])] if k >= 3 else []
            tl = dma(buf[:], wqkv[128 * k:128 * (k + 1), 2048:3072], wsem, wait=wb)
            for half in range(2):
                tkh = pe(lambda E, k=k, half=half, buf=buf: E.matmul(
                    pb[2 + half][:], xh_fm[:, 128 * k:128 * k + 128],
                    buf[:, 512 * half:512 * half + 512],
                    start=(k == 0), stop=(k == NCH - 1)),
                    wait=[(wsem, tl), ("ve", t_xhfm)] + (bw(2 + half) if k == 0 else []))
            for half in range(2):
                tkg2 = pe(lambda E, k=k, half=half, buf=buf: E.matmul(
                    pb[4 + half][0:64, :], xg_fm[:, 64 * k:64 * k + 64],
                    buf[:, 512 * half:512 * half + 512],
                    start=(k == 0), stop=(k == NCH - 1)),
                    wait=[("ve", t_xgfm)] + (bw(4 + half) if k == 0 else []))
            wrow_used[k] = tkg2
        wrow_prev_last3 = [wrow_used[6], wrow_used[7], wrow_used[5]]
        t_vp = t_vg = None
        for half in range(2):
            t_vp = ve(lambda E, half=half: E.tensor_copy(
                vprev[:, 512 * half:512 * half + 512], pb[2 + half][:]),
                wait=[("pe", tkh)])
            bankfree[2 + half] = ("ve", t_vp)
        for half in range(2):
            t_vg = ve(lambda E, half=half: E.tensor_copy(
                vglob[0:64, 512 * half:512 * half + 512], pb[4 + half][0:64, :]),
                wait=[("pe", tkg2)])
            bankfree[4 + half] = ("ve", t_vg)
        prev_tok = max(t_kp, t_vp)

        t_wfin = None
        t_wb_cons = [None, None]
        for dst, asrc, bsrc in ((wtr, aer_in, ber_in), (wti, aei_in, bei_in)):
            for k in range(4):
                for c in range(4):
                    wa = [("ve", t_wb_cons[0])] if t_wb_cons[0] else []
                    wb_ = [("ve", t_wb_cons[1])] if t_wb_cons[1] else []
                    ta = gdma(wbuf[0][:], asrc[c, 128 * k:128 * k + 128, :],
                              "wbA", wait=wa)
                    tb = gdma(wbuf[1][:], bsrc[c, 128 * k:128 * k + 128, :],
                              "wbB", wait=wb_)
                    if c == 0:
                        t_wfin = ve(lambda E, dst=dst, k=k: E.tensor_tensor(
                            dst[:, 512 * k:512 * (k + 1)],
                            wbuf[0][:], wbuf[1][:], ALU.mult),
                            wait=[("wbA", ta), ("wbB", tb)])
                    else:
                        t_wfin = ve(lambda E: E.tensor_tensor(
                            prodscr[:, 0:512], wbuf[0][:], wbuf[1][:], ALU.mult),
                            wait=[("wbA", ta), ("wbB", tb)])
                        t_wfin = ve(lambda E, dst=dst, k=k: E.tensor_tensor(
                            dst[:, 512 * k:512 * (k + 1)],
                            dst[:, 512 * k:512 * (k + 1)], prodscr[:, 0:512],
                            ALU.add))
                    t_wb_cons = [t_wfin, t_wfin]
        for k in range(4):
            t_wfin = ve(lambda E, k=k: E.tensor_scalar(
                wtn[:, 512 * k:512 * (k + 1)], wti[:, 512 * k:512 * (k + 1)],
                cpool[:, C_NEG1:C_NEG1 + 1], None, ALU.mult),
                wait=[("const", T_CONST)])

        t_xload = {}
        for sub in range(NSUB):
            wpro = [("ve", t_xgfm)] if sub < 2 else []
            t_xload[(0, sub)] = dma(x_tm[sub][:],
                                    xm[HALO + 128 * sub:HALO + 128 * (sub + 1), :],
                                    f"x{sub}", wait=wpro)

        if dbg:
            store(dbg_t["dbg_wtr"], wtr[:], wait=[("ve", t_wfin)])
            store(dbg_t["dbg_wti"], wti[:], wait=[("ve", t_wfin)])
            store(dbg_t["dbg_vglob"], vglob[0:64, :], wait=[("ve", t_vg)])
            store(dbg_t["dbg_vprev"], vprev[:], wait=[("ve", t_vp)])

        t_sqscr_rd = None
        t_attn_done_prev = None
        t_proj_done_prev = None
        t_ffn_done_prev = None
        t_resid = {}
        t_store = {}
        t_ln1nrm_all = {}
        t_ln2nrm_all = {}
        wout_prev_last3 = wrow_prev_last3

        for t in range(nt):
            xb = [x_tm[s] for s in range(NSUB)]

            w_ffnprev = [("pe", t_ffn_done_prev)] if t_ffn_done_prev else []
            t_xT = None
            for sub in range(NSUB):
                for g in range(2):
                    bank = 6 + g
                    tk = None
                    for c in range(4 * g, 4 * g + 4):
                        tk = pe(lambda E, c=c, bank=bank, x=xb[sub]: E.transpose(
                            pb[bank][:, 128 * (c % 4):128 * (c % 4) + 128],
                            x[:, 128 * c:128 * c + 128], ident[:]),
                            wait=[(f"x{sub}", t_xload[(t, sub)])] + bw(bank))
                    for c in range(4 * g, 4 * g + 4):
                        t_xT = ve(lambda E, c=c, sub=sub, bank=bank: E.tensor_copy(
                            xT[:, 512 * c + 128 * sub:512 * c + 128 * sub + 128],
                            pb[bank][:, 128 * (c % 4):128 * (c % 4) + 128]),
                            wait=[("pe", tk)] + w_ffnprev)
                    bankfree[bank] = ("ve", t_xT)

            w_attnprev = [("pe", t_attn_done_prev)] if t_attn_done_prev else []
            t_q_tok = {}
            t_k_tok = {}
            for m in range(16):
                buf = wcol[m % 2]
                wsem = f"wc{m % 2}"
                wb = [("pe", wcol_used[m - 2])] if m >= 2 else \
                     [("pe", wcol_prev_last2[m])]
                tl = dma(buf[:], wcol_src(m), wsem, wait=wb)
                bank = m % 2
                tk = None
                for k in range(NCH):
                    tk = pe(lambda E, k=k, buf=buf, bank=bank: E.matmul(
                        pb[bank][:], buf[:, 128 * k:128 * k + 128],
                        xT[:, 512 * k:512 * (k + 1)],
                        start=(k == 0), stop=(k == NCH - 1)),
                        wait=[(wsem, tl), ("ve", t_xT)] + (bw(bank) if k == 0 else []))
                wcol_used[m] = tk
                if m < 8:
                    tq = sc(lambda E, m=m, bank=bank: E.activation(
                        qfm[:, 512 * m:512 * (m + 1)], pb[bank][:],
                        AF.Copy, scale=0.125),
                        wait=[("pe", tk)] + w_attnprev)
                    t_q_tok[m] = tq
                    bankfree[bank] = ("sc", tq)
                else:
                    tkk = ve(lambda E, m=m, bank=bank: E.tensor_copy(
                        kwin[:, KW * (m - 8) + 192:KW * (m - 8) + KW], pb[bank][:]),
                        wait=[("pe", tk), ("ve", prev_tok)] + w_attnprev)
                    t_k_tok[m - 8] = tkk
                    bankfree[bank] = ("ve", tkk)
            wcol_prev_last2 = [wcol_used[14], wcol_used[15]]

            t_v_tok = {}
            for k in range(NCH):
                buf = wrow[k % 3]
                wsem = f"wr{k % 3}"
                if k >= 3:
                    wb = [("pe", wrow_used[k - 3])]
                elif t == 0:
                    wb = [("pe", wrow_prev_last3[k])]
                else:
                    wb = [("pe", wout_prev_last3[k])]
                tl = dma(buf[:], wqkv[128 * k:128 * (k + 1), 2048:3072], wsem, wait=wb)
                tk = None
                for s in range(NSUB):
                    for half in range(2):
                        bank = 2 * s + half
                        tk = pe(lambda E, k=k, s=s, half=half, buf=buf, bank=bank:
                                E.matmul(
                                    pb[bank][:],
                                    xT[:, 512 * k + 128 * s:512 * k + 128 * s + 128],
                                    buf[:, 512 * half:512 * half + 512],
                                    start=(k == 0), stop=(k == NCH - 1)),
                                wait=[(wsem, tl), ("ve", t_xT)] +
                                     (bw(bank) if k == 0 else []))
                wrow_used[k] = tk
            for s in range(NSUB):
                for half in range(2):
                    bank = 2 * s + half
                    tvv = ve(lambda E, s=s, half=half, bank=bank: E.tensor_copy(
                        vcur[s][:, 512 * half:512 * half + 512], pb[bank][:]),
                        wait=[("pe", wrow_used[NCH - 1]), ("ve", prev_tok)] + w_attnprev)
                    t_v_tok[s] = tvv
                    bankfree[bank] = ("ve", tvv)

            w_projprev = [("pe", t_proj_done_prev)] if t_proj_done_prev else []
            seq = [(pair, r, hh) for pair in range(8) for r in range(NSUB)
                   for hh in range(2)]
            NIT = len(seq)
            DEPTH = 2
            tok_exp = {}
            tok_nm = {}
            tok_trans = {}
            tok_tpe = {}
            tok_av = {}
            t_ao_tok = {}
            t_attn_done = None

            def emit_head(i):
                pair, r, hh = seq[i]
                midx = min(4 * t + r, 2)
                i3 = i % 3
                sb = i % 3
                q_ap = qfm[64 * hh:64 * hh + 64,
                           512 * pair + 128 * r:512 * pair + 128 * r + 128]
                wsc = [("sc", t_q_tok[pair]), ("ve", t_k_tok[pair]),
                       ("ve", prev_tok)] + bw(sb)
                if r == 0:
                    tk1 = pe(lambda E, q_ap=q_ap, hh=hh, pair=pair, sb=sb: E.matmul(
                        pb[sb][:, 0:320], q_ap,
                        kwin[64 * hh:64 * hh + 64, KW * pair:KW * pair + 320],
                        start=True, stop=True), wait=wsc)
                else:
                    tk1 = pe(lambda E, q_ap=q_ap, hh=hh, pair=pair, sb=sb: E.matmul(
                        pb[sb][:, 0:64], q_ap,
                        kwin[64 * hh:64 * hh + 64, KW * pair:KW * pair + 64],
                        start=True, stop=True), wait=wsc)
                    tk1 = pe(lambda E, q_ap=q_ap, hh=hh, pair=pair, r=r, sb=sb:
                             E.matmul(
                                 pb[sb][:, 64:320], q_ap,
                                 kwin[64 * hh:64 * hh + 64,
                                      KW * pair + 64 + 128 * r:
                                      KW * pair + 64 + 128 * r + 256],
                                 start=True, stop=True))
                tmx = ve(lambda E, sb=sb, i3=i3: E.reduce_max(
                    smax[i3][:], pb[sb][:, 0:320], axis=AX.X, negate=True),
                    wait=[("pe", tk1)] +
                         ([("sc", tok_exp[i - 3])] if i - 3 in tok_exp else []))
                tex = sc(lambda E, sb=sb, i3=i3: E.activation(
                    pexp[i3][:], pb[sb][:, 0:320], AF.Exp,
                    bias=smax[i3][:], scale=1.0),
                    wait=[("ve", tmx)] +
                         ([("pe", tok_trans[i - 3])] if i - 3 in tok_trans else []))
                tok_exp[i] = tex
                bankfree[sb] = ("sc", tex)
                tms = ve(lambda E, i3=i3, midx=midx: E.scalar_tensor_tensor(
                    pexp[i3][:], pexp[i3][:], cpool[:, C_ONE:C_ONE + 1],
                    masks[:, 320 * midx:320 * (midx + 1)],
                    ALU.mult, ALU.mult, accum_out=ssum[i3][:]),
                    wait=[("sc", tex), ("const", T_CONST)])
                trec = ve(lambda E, i3=i3: E.reciprocal(
                    srinv[i3][:], ssum[i3][:]), wait=[("ve", tms)])
                tnm = ve(lambda E, i3=i3: E.tensor_scalar(
                    pexp[i3][:], pexp[i3][:], srinv[i3][:], None, ALU.mult),
                    wait=[("ve", trec)])
                tok_nm[i] = tnm

            def emit_tail(j):
                nonlocal t_attn_done
                pair, r, hh = seq[j]
                h = 2 * pair + hh
                j3 = j % 3
                tb = 3 if j % 2 == 0 else 6
                av_bank = 4 + (pair % 2)
                tk2 = pe(lambda E, j3=j3, tb=tb: E.transpose(
                    pb[tb][0:64, 0:128], pexp[j3][:, 0:64], ident[:]),
                    wait=[("ve", tok_nm[j])] + bw(tb))
                tk2 = pe(lambda E, j3=j3, tb=tb: E.transpose(
                    pb[tb][:, 128:256], pexp[j3][:, 64:192], ident[:]))
                tk2 = pe(lambda E, j3=j3, tb=tb: E.transpose(
                    pb[tb][:, 256:384], pexp[j3][:, 192:320], ident[:]))
                tok_trans[j] = tk2
                tpe = sc(lambda E, j3=j3, j2=j % 2, tb=tb: E.copy(
                    pT_sb[j2][:], pb[tb][:, 0:384]),
                    wait=[("pe", tk2)] +
                         ([("pe", tok_av[j - 2])] if j - 2 in tok_av else []))
                tok_tpe[j] = tpe
                bankfree[tb] = ("sc", tpe)
                tp_av = (0, 64) if hh else None
                wav = [("sc", tpe), ("ve", t_v_tok[r]), ("ve", t_vg),
                       ("ve", t_vp)] + \
                      ([("ve", t_v_tok[r - 1])] if r >= 1 else []) + \
                      (bw(av_bank) if (r == 0 and hh == 0) else [])
                vband = vprev if r == 0 else vcur[r - 1]
                j2 = j % 2
                tk3 = pe(lambda E, h=h, hh=hh, r=r, j2=j2, ab=av_bank,
                         tp=tp_av: E.matmul(
                             pb[ab][64 * hh:64 * hh + 64, 128 * r:128 * r + 128],
                             vglob[0:64, 64 * h:64 * h + 64],
                             pT_sb[j2][0:64, 0:128],
                             start=True, stop=False, tile_position=tp),
                         wait=wav)
                tk3 = pe(lambda E, h=h, hh=hh, r=r, j2=j2, ab=av_bank,
                         vb=vband, tp=tp_av: E.matmul(
                             pb[ab][64 * hh:64 * hh + 64, 128 * r:128 * r + 128],
                             vb[:, 64 * h:64 * h + 64],
                             pT_sb[j2][:, 128:256],
                             start=False, stop=False, tile_position=tp))
                tk3 = pe(lambda E, h=h, hh=hh, r=r, j2=j2, ab=av_bank,
                         tp=tp_av: E.matmul(
                             pb[ab][64 * hh:64 * hh + 64, 128 * r:128 * r + 128],
                             vcur[r][:, 64 * h:64 * h + 64],
                             pT_sb[j2][:, 256:384],
                             start=False, stop=True, tile_position=tp))
                tok_av[j] = tk3
                t_attn_done = tk3
                if r == NSUB - 1 and hh == 1:
                    tao = ve(lambda E, pair=pair, ab=av_bank: E.tensor_copy(
                        aoT[:, 512 * pair:512 * (pair + 1)], pb[ab][:]),
                        wait=[("pe", tk3)] + w_projprev)
                    t_ao_tok[pair] = tao
                    bankfree[av_bank] = ("ve", tao)

            for step in range(NIT + DEPTH):
                if step < NIT:
                    emit_head(step)
                if step >= DEPTH:
                    emit_tail(step - DEPTH)

            if t + 1 < nt:
                tpc = None
                for c in range(NCH):
                    tpc = ve(lambda E, c=c: E.tensor_copy(
                        kwin[:, KW * c + 64:KW * c + 192],
                        kwin[:, KW * c + 192 + 384:KW * c + KW]),
                        wait=[("pe", t_attn_done)])
                tpv = ve(lambda E: E.tensor_copy(vprev[:], vcur[3][:]),
                         wait=[("pe", t_attn_done)])
                prev_tok = tpv

            t_wout_load = {}
            t_wout_used = {}
            t_proj_bank = {}
            for k in range(NCH):
                buf = wrow[k % 3]
                wsem = f"wr{k % 3}"
                if k >= 3:
                    wb = [("pe", t_wout_used[k - 3])]
                else:
                    wb = [("pe", wrow_used[{0: 6, 1: 7, 2: 5}[k]])]
                tl = dma(buf[:], wout[128 * k:128 * (k + 1), :], wsem, wait=wb)
                t_wout_load[k] = tl
                tk = None
                for r in range(NSUB):
                    for half in range(2):
                        bank = 2 * r + half
                        tk = pe(lambda E, k=k, r=r, half=half, buf=buf, bank=bank:
                                E.matmul(
                                    pb[bank][:],
                                    aoT[:, 512 * k + 128 * r:512 * k + 128 * r + 128],
                                    buf[:, 512 * half:512 * half + 512],
                                    start=(k == 0), stop=(k == NCH - 1)),
                                wait=[(wsem, tl), ("ve", t_ao_tok[k])] +
                                     (bw(bank) if k == 0 else []))
                        if k == NCH - 1:
                            t_proj_bank[(r, half)] = tk
                t_wout_used[k] = tk
            wout_prev_last3 = [t_wout_used[6], t_wout_used[7], t_wout_used[5]]
            t_proj_done = t_wout_used[NCH - 1]
            t_ln1pre = {}
            for r in range(NSUB):
                tres = None
                for half in range(2):
                    bank = 2 * r + half
                    tres = ve(lambda E, r=r, half=half, bank=bank, x=xb[r]:
                              E.scalar_tensor_tensor(
                                  ln1tm[r][:, 512 * half:512 * half + 512],
                                  pb[bank][:], cpool[:, C_ONE:C_ONE + 1],
                                  x[:, 512 * half:512 * half + 512],
                                  ALU.mult, ALU.add),
                              wait=[("pe", t_proj_bank[(r, half)])])
                    bankfree[bank] = ("ve", tres)
                t_ln1pre[r] = tres
            t_resid[t] = t_ln1pre[NSUB - 1]
            if t + 1 < nt:
                for sub in range(NSUB):
                    t_xload[(t + 1, sub)] = dma(
                        x_tm[sub][:],
                        xm[HALO + 512 * (t + 1) + 128 * sub:
                           HALO + 512 * (t + 1) + 128 * (sub + 1), :],
                        f"x{sub}", wait=[("ve", t_ln1pre[sub])])

            def ln_stats(si_, inp_fn, pre_wait, wpar):
                nonlocal t_sqscr_rd
                tsqr = sc(lambda E: E.activation(
                    sqscr[:], inp_fn(), AF.Square),
                    wait=pre_wait + wpar +
                         ([("ve", t_sqscr_rd)] if t_sqscr_rd else []))
                tsq = ve(lambda E, si_=si_: E.reduce_sum(
                    s_sq[si_][:], sqscr[:], axis=AX.X), wait=[("sc", tsqr)])
                t_sqscr_rd = tsq
                tred = ve(lambda E, si_=si_: E.reduce_sum(
                    s_sum[si_][:], inp_fn(), axis=AX.X))
                tt = ve(lambda E, si_=si_: E.tensor_scalar(
                    s_mean[si_][:], s_sum[si_][:], cpool[:, C_INV1024:C_INV1024 + 1],
                    None, ALU.mult), wait=[("const", T_CONST), ("ve", tred)])
                tt = ve(lambda E, si_=si_: E.tensor_scalar(
                    s_msq[si_][:], s_sq[si_][:], cpool[:, C_INV1024:C_INV1024 + 1],
                    None, ALU.mult), wait=[("ve", tsq), ("ve", tt)])
                tt = ve(lambda E, si_=si_: E.tensor_tensor(
                    s_var[si_][:], s_mean[si_][:], s_mean[si_][:], ALU.mult),
                    wait=[("ve", tt)])
                tt = ve(lambda E, si_=si_: E.tensor_tensor(
                    s_var[si_][:], s_msq[si_][:], s_var[si_][:], ALU.subtract),
                    wait=[("ve", tt)])
                tvar = ve(lambda E, si_=si_: E.tensor_scalar(
                    s_var[si_][:], s_var[si_][:], cpool[:, C_EPS:C_EPS + 1],
                    None, ALU.add), wait=[("ve", tt)])
                tsd = sc(lambda E, si_=si_: E.activation(
                    s_sd[si_][:], s_var[si_][:], AF.Sqrt), wait=[("ve", tvar)])
                trc = ve(lambda E, si_=si_: E.reciprocal(
                    s_rs[si_][:], s_sd[si_][:]), wait=[("sc", tsd)])
                tsh = ve(lambda E, si_=si_: E.scalar_tensor_tensor(
                    s_sh[si_][:], s_mean[si_][:], cpool[:, C_NEG1:C_NEG1 + 1],
                    s_rs[si_][:], ALU.mult, ALU.mult), wait=[("ve", trc)])
                return tsh

            t_ln1nrm = {}
            t_ln2nrm = {}
            t_ffn_done = None
            for r in range(NSUB):
                si_ = r % 2
                if (r - 2) in t_ln1nrm:
                    wpar = [("ve", t_ln1nrm[r - 2])]
                elif t > 0:
                    wpar = [("ve", t_ln1nrm_all[t - 1][r + 2])]
                else:
                    wpar = []
                tsh = ln_stats(si_, (lambda r=r: ln1tm[r][:]),
                               [("ve", t_ln1pre[r])], wpar)
                tnm = ve(lambda E, r=r, si_=si_: E.tensor_scalar(
                    ln1tm[r][:], ln1tm[r][:], s_rs[si_][:], s_sh[si_][:],
                    ALU.mult, ALU.add), wait=[("ve", tsh)])
                t_ln1nrm[r] = tnm
                t_ln1fm_r = None
                for g in range(2):
                    bank = 2 + g
                    tk = None
                    for c in range(4 * g, 4 * g + 4):
                        tk = pe(lambda E, c=c, r=r, bank=bank: E.transpose(
                            pb[bank][:, 128 * (c % 4):128 * (c % 4) + 128],
                            ln1tm[r][:, 128 * c:128 * c + 128], ident[:]),
                            wait=[("ve", tnm)] + bw(bank))
                    for c in range(4 * g, 4 * g + 4):
                        t_ln1fm_r = ve(lambda E, c=c, r=r, bank=bank: E.tensor_copy(
                            xT[:, 512 * c + 128 * r:512 * c + 128 * r + 128],
                            pb[bank][:, 128 * (c % 4):128 * (c % 4) + 128]),
                            wait=[("pe", tk)])
                    bankfree[bank] = ("ve", t_ln1fm_r)
                banks = (4, 5) if r % 2 == 0 else (6, 7)
                tk = None
                for c in range(4):
                    lhs = xT[:, 512 * c + 128 * r:512 * c + 128 * r + 128]
                    tk = pe(lambda E, lhs=lhs, c=c, bank=banks[0]: E.matmul(
                        pb[bank][:], lhs, wtr[:, 512 * c:512 * (c + 1)],
                        start=(c == 0), stop=False),
                        wait=[("ve", t_ln1fm_r), ("ve", t_wfin)] +
                             (bw(banks[0]) if c == 0 else []))
                    tk = pe(lambda E, lhs=lhs, c=c, bank=banks[1]: E.matmul(
                        pb[bank][:], lhs, wti[:, 512 * c:512 * (c + 1)],
                        start=(c == 0), stop=False),
                        wait=(bw(banks[1]) if c == 0 else []))
                for c in range(4):
                    lhs = xT[:, 512 * (4 + c) + 128 * r:512 * (4 + c) + 128 * r + 128]
                    tk = pe(lambda E, lhs=lhs, c=c, bank=banks[0]: E.matmul(
                        pb[bank][:], lhs, wtn[:, 512 * c:512 * (c + 1)],
                        start=False, stop=False))
                    tk = pe(lambda E, lhs=lhs, c=c, bank=banks[1]: E.matmul(
                        pb[bank][:], lhs, wtr[:, 512 * c:512 * (c + 1)],
                        start=False, stop=False))
                tk = pe(lambda E, bank=banks[0]: E.matmul(
                    pb[bank][:], ones1[:], fbias[:, 0:512],
                    start=False, stop=True), wait=[("const", T_CONST)])
                tk = pe(lambda E, bank=banks[1]: E.matmul(
                    pb[bank][:], ones1[:], fbias[:, 512:1024],
                    start=False, stop=True))
                t_ffn_done = tk
                ob = otm[r % 2]
                stsem = f"st{r % 2}"
                if (t, r - 2) in t_store:
                    wst = [(stsem, t_store[(t, r - 2)])]
                elif (t - 1, r + 2) in t_store:
                    wst = [(stsem, t_store[(t - 1, r + 2)])]
                else:
                    wst = []
                tpre = None
                for half in range(2):
                    bank = banks[half]
                    tpre = ve(lambda E, r=r, half=half, bank=bank, ob=ob:
                              E.scalar_tensor_tensor(
                                  ob[:, 512 * half:512 * half + 512],
                                  pb[bank][:], cpool[:, C_ONE:C_ONE + 1],
                                  ln1tm[r][:, 512 * half:512 * half + 512],
                                  ALU.mult, ALU.add),
                              wait=[("pe", tk)] + wst)
                    bankfree[bank] = ("ve", tpre)
                si2 = 2 + r % 2
                if (r - 2) in t_ln2nrm:
                    wpar = [("ve", t_ln2nrm[r - 2])]
                elif t > 0:
                    wpar = [("ve", t_ln2nrm_all[t - 1][r + 2])]
                else:
                    wpar = []
                tsh = ln_stats(si2, (lambda ob=ob: ob[:]), [("ve", tpre)], wpar)
                tnm = ve(lambda E, ob=ob, si2=si2: E.tensor_scalar(
                    ob[:], ob[:], s_rs[si2][:], s_sh[si2][:],
                    ALU.mult, ALU.add), wait=[("ve", tsh)])
                t_ln2nrm[r] = tnm
                t_store[(t, r)] = store(
                    y[512 * t + 128 * r:512 * t + 128 * (r + 1), :], ob[:],
                    sem=stsem, wait=[("ve", tnm)])
            t_ln1nrm_all[t] = t_ln1nrm
            t_ln2nrm_all[t] = t_ln2nrm

            if dbg and t == 0:
                store(dbg_t["dbg_qfm"], qfm[:], wait=[("sc", t_q_tok[7])])
                store(dbg_t["dbg_kwin"], kwin[:], wait=[("ve", t_k_tok[7])])
                for s in range(NSUB):
                    store(dbg_t["dbg_vcur"][s], vcur[s][:], wait=[("ve", t_v_tok[s])])
                store(dbg_t["dbg_aot"], aoT[:], wait=[("ve", t_ao_tok[7])])
                for r in range(NSUB):
                    store(dbg_t["dbg_ln1"][r], ln1tm[r][:],
                          wait=[("ve", t_ln1nrm[r])])
                store(dbg_t["dbg_xt"], xT[:], wait=[("ve", t_ln1fm_r)])

            t_attn_done_prev = t_attn_done
            t_proj_done_prev = t_proj_done
            t_ffn_done_prev = t_ffn_done

        p.emit("gpsimd", None,
               wait=[(s, p.ctr[s]) for s in ("st0", "st1", "dbgst")
                     if p.ctr[s] > 0])

        globals()["_last_plan"] = dict(p.ctr)

        with nc.Block() as block:
            def mk(eng_name):
                def body(E):
                    for waits, fn, inc, amt in p.ops[eng_name]:
                        for s, v in waits:
                            E.wait_ge(sems[s], v)
                        if fn is not None:
                            inst = fn(E)
                            if inc:
                                inst.then_inc(sems[inc], amt)
                return body
            block.sync(mk("sync"))
            block.tensor(mk("tensor"))
            block.vector(mk("vector"))
            block.scalar(mk("scalar"))
            block.gpsimd(mk("gpsimd"))

    return nc



def make_core_inputs(x, w_qkv, w_out, G1_R, G2_R, G1_I, G2_I, b_r, b_i):
    x = np.ascontiguousarray(np.asarray(x, dtype=np.float32))
    w_qkv = np.ascontiguousarray(np.asarray(w_qkv, dtype=np.float32))
    w_out = np.ascontiguousarray(np.asarray(w_out, dtype=np.float32))
    r_idx = np.arange(128)[:, None]
    c_idx = np.arange(128)[None, :]
    low = (c_idx <= r_idx).astype(np.float32)
    up = (c_idx >= r_idx).astype(np.float32)
    up64 = ((c_idx >= r_idx) | (c_idx < 64)).astype(np.float32)
    g0 = np.zeros((128, 64), np.float32)
    g1 = np.ones((128, 64), np.float32)
    z128 = np.zeros((128, 128), np.float32)
    masks_h0 = np.ascontiguousarray(np.stack([
        np.concatenate([g0, z128, low], axis=1),
        np.concatenate([g0, up64, low], axis=1),
        np.concatenate([g1, up, low], axis=1)]))
    m2 = np.concatenate([g1, up, low], axis=1)
    masks_h1 = np.ascontiguousarray(np.stack([m2, m2, m2]))
    ident = np.eye(128, dtype=np.float32)
    cpool = np.zeros((128, 8), np.float32)
    cpool[:, C_INV1024] = 1.0 / 1024.0
    cpool[:, C_EPS] = 1e-5
    cpool[:, C_NEG1] = -1.0
    cpool[:, C_ONE] = 1.0

    oi = np.arange(512)
    aexp = {}
    bexp = {}
    for tag, G1, G2 in (("r", G1_R, G2_R), ("i", G1_I, G2_I)):
        G1 = np.asarray(G1, np.float32)
        G2 = np.asarray(G2, np.float32)
        a = np.empty((4, 512, 512), np.float32)
        b = np.empty((4, 512, 512), np.float32)
        for c in range(4):
            A = G1[0, :, c, :]
            Bm = G2[c, :, 0, :]
            a[c] = A[oi // 16, :][:, oi // 16].T
            b[c] = Bm[oi % 16, :][:, oi % 16].T
        aexp[tag] = np.ascontiguousarray(a)
        bexp[tag] = np.ascontiguousarray(b)
    fbias = np.ascontiguousarray(
        np.concatenate([np.asarray(b_r, np.float32),
                        np.asarray(b_i, np.float32)])[None, :])

    in_maps = []
    for core in range(8):
        b = core // 2
        half = core % 2
        if half == 0:
            x_main = np.concatenate([np.zeros((HALO, D), np.float32),
                                     x[b, 0:OWN]], axis=0)
        else:
            x_main = x[b, OWN - HALO:SEQ]
        in_maps.append({
            "x_main": np.ascontiguousarray(x_main),
            "x_glob": np.ascontiguousarray(x[b, 0:GLOB]),
            "w_qkv": w_qkv,
            "w_out": w_out,
            "masks": masks_h0 if half == 0 else masks_h1,
            "ident": ident,
            "cpool": cpool,
            "aexp_r": aexp["r"], "bexp_r": bexp["r"],
            "aexp_i": aexp["i"], "bexp_i": bexp["i"],
            "fbias": fbias,
        })
    return in_maps


def kernel(L0_attn_x, L0_attn_w_qkv, L0_attn_w_out, L0_ffn_G1_R, L0_ffn_G2_R,
           L0_ffn_G1_I, L0_ffn_G2_I, L0_ffn_bias_real, L0_ffn_bias_imag):
    in_maps = make_core_inputs(L0_attn_x, L0_attn_w_qkv, L0_attn_w_out,
                               L0_ffn_G1_R, L0_ffn_G2_R, L0_ffn_G1_I,
                               L0_ffn_G2_I, L0_ffn_bias_real, L0_ffn_bias_imag)
    nc = build_program(dbg=False, nt=NT)
    res = run_bass_kernel_spmd(nc, in_maps, core_ids=list(range(8)))
    out = np.empty((NB, SEQ, D), dtype=np.float32)
    for core in range(8):
        b, half = core // 2, core % 2
        out[b, OWN * half:OWN * (half + 1)] = res.results[core]["y"]
    return out


# version 03
# speedup vs baseline: 1.2876x; 1.0993x over previous
import numpy as np
from contextlib import ExitStack
import concourse.bass as bass
import concourse.mybir as mybir
from concourse.bass_utils import run_bass_kernel_spmd

F32 = mybir.dt.float32
AF = mybir.ActivationFunctionType
ALU = mybir.AluOpType
AX = mybir.AxisListType

D = 1024
SEQ = 4096
NB = 4
NSUB = 4
NT = 4
OWN = 2048
HALO = 128
GLOB = 64
NCH = 8
KW = 704

C_INV1024 = 0
C_EPS = 1
C_NEG1 = 2
C_ONE = 3

ENGS = ("sync", "tensor", "vector", "scalar", "gpsimd")

DMA_SEMS = ("const", "x0", "x1", "x2", "x3", "wc0", "wc1",
            "wr0", "wr1", "wr2", "wbA", "wbB", "st0", "st1", "dbgst")


class Plan:
    def __init__(self):
        self.ops = {e: [] for e in ENGS}
        self.ctr = {n: 0 for n in ("pe", "ve", "sc") + DMA_SEMS}
        self.lastw = {e: {} for e in ENGS}

    def emit(self, eng, fn, wait=(), inc=None):
        w = []
        for sem, val in wait:
            if not val:
                continue
            if val > self.lastw[eng].get(sem, 0):
                w.append((sem, val))
                self.lastw[eng][sem] = val
        amt = 0
        if inc:
            amt = 16 if inc in DMA_SEMS else 1
            self.ctr[inc] += amt
        self.ops[eng].append((tuple(w), fn, inc, amt))
        return self.ctr[inc] if inc else None


def build_program(dbg=False, nt=NT):
    nc = bass.Bass("TRN2", target_bir_lowering=False)

    xm = nc.dram_tensor("x_main", [HALO + OWN, D], F32, kind="ExternalInput").ap()
    xg_in = nc.dram_tensor("x_glob", [GLOB, D], F32, kind="ExternalInput").ap()
    wqkv = nc.dram_tensor("w_qkv", [D, 3 * D], F32, kind="ExternalInput").ap()
    wout = nc.dram_tensor("w_out", [D, D], F32, kind="ExternalInput").ap()
    masks_in = nc.dram_tensor("masks", [3, 128, 320], F32, kind="ExternalInput").ap()
    ident_in = nc.dram_tensor("ident", [128, 128], F32, kind="ExternalInput").ap()
    cpool_in = nc.dram_tensor("cpool", [128, 8], F32, kind="ExternalInput").ap()
    aer_in = nc.dram_tensor("aexp_r", [4, 512, 512], F32, kind="ExternalInput").ap()
    ber_in = nc.dram_tensor("bexp_r", [4, 512, 512], F32, kind="ExternalInput").ap()
    aei_in = nc.dram_tensor("aexp_i", [4, 512, 512], F32, kind="ExternalInput").ap()
    bei_in = nc.dram_tensor("bexp_i", [4, 512, 512], F32, kind="ExternalInput").ap()
    bias_in = nc.dram_tensor("fbias", [1, 1024], F32, kind="ExternalInput").ap()
    y = nc.dram_tensor("y", [OWN, D], F32, kind="ExternalOutput").ap()
    dbg_t = {}
    if dbg:
        for name, shape in (("dbg_qfm", [128, 4096]), ("dbg_kwin", [128, 8 * KW]),
                            ("dbg_vcur", [4, 128, 1024]),
                            ("dbg_vglob", [64, 1024]), ("dbg_vprev", [128, 1024]),
                            ("dbg_aot", [128, 4096]), ("dbg_ln1", [4, 128, 1024]),
                            ("dbg_wtr", [128, 2048]), ("dbg_wti", [128, 2048]),
                            ("dbg_xt", [128, 4096])):
            dbg_t[name] = nc.dram_tensor(name, shape, F32, kind="ExternalOutput").ap()

    p = Plan()

    with ExitStack() as ctx:
        e = ctx.enter_context
        ident = e(nc.sbuf_tensor("ident_sb", [128, 128], F32))
        cpool = e(nc.sbuf_tensor("cpool_sb", [128, 8], F32))
        masks = e(nc.sbuf_tensor("masks_sb", [128, 3 * 320], F32))
        x_tm = [e(nc.sbuf_tensor(f"x_tm{i}", [128, D], F32)) for i in range(4)]
        xh_tm = x_tm[0]
        xg_tm = x_tm[1]
        xh_fm = e(nc.sbuf_tensor("xh_fm", [128, 1024], F32))
        xg_fm = e(nc.sbuf_tensor("xg_fm", [128, 512], F32))
        xT = e(nc.sbuf_tensor("xT", [128, 4096], F32))
        qfm = e(nc.sbuf_tensor("qfm", [128, 4096], F32))
        kwin = e(nc.sbuf_tensor("kwin", [128, 8 * KW], F32))
        vcur = [e(nc.sbuf_tensor(f"vcur{s}", [128, D], F32)) for s in range(4)]
        vprev = e(nc.sbuf_tensor("vprev", [128, D], F32))
        vglob = e(nc.sbuf_tensor("vglob", [64, D], F32))
        aoT = e(nc.sbuf_tensor("aoT", [128, 4096], F32))
        ln1tm = [e(nc.sbuf_tensor(f"ln1tm{s}", [128, D], F32)) for s in range(4)]
        otm = [e(nc.sbuf_tensor(f"otm{i}", [128, D], F32)) for i in range(2)]
        sqscr = xh_fm
        pexp = [e(nc.sbuf_tensor(f"pexp{i}", [128, 320], F32)) for i in range(3)]
        pT_sb = [e(nc.sbuf_tensor(f"pT{i}", [128, 384], F32)) for i in range(2)]
        wcol = [e(nc.sbuf_tensor(f"wcol{i}", [128, 1024], F32)) for i in range(2)]
        wrow = [e(nc.sbuf_tensor(f"wrow{i}", [128, 1024], F32)) for i in range(3)]
        wbuf = [e(nc.sbuf_tensor(f"wbuf{i}", [128, 512], F32)) for i in range(2)]
        wtr = e(nc.sbuf_tensor("wtr", [128, 2048], F32))
        wti = e(nc.sbuf_tensor("wti", [128, 2048], F32))
        wtn = e(nc.sbuf_tensor("wtn", [128, 2048], F32))
        fbias = e(nc.sbuf_tensor("fbias_sb", [1, 1024], F32))
        ones1 = e(nc.sbuf_tensor("ones1", [1, 128], F32))
        smax = [e(nc.sbuf_tensor(f"smax{i}", [128, 1], F32)) for i in range(3)]
        ssum = [e(nc.sbuf_tensor(f"ssum{i}", [128, 1], F32)) for i in range(3)]
        srinv = [e(nc.sbuf_tensor(f"srinv{i}", [128, 1], F32)) for i in range(3)]
        s_sum = [e(nc.sbuf_tensor(f"s_sum{i}", [128, 1], F32)) for i in range(4)]
        s_sq = [e(nc.sbuf_tensor(f"s_sq{i}", [128, 1], F32)) for i in range(4)]
        s_mean = [e(nc.sbuf_tensor(f"s_mean{i}", [128, 1], F32)) for i in range(4)]
        s_msq = [e(nc.sbuf_tensor(f"s_msq{i}", [128, 1], F32)) for i in range(4)]
        s_var = [e(nc.sbuf_tensor(f"s_var{i}", [128, 1], F32)) for i in range(4)]
        s_sd = [e(nc.sbuf_tensor(f"s_sd{i}", [128, 1], F32)) for i in range(4)]
        s_rs = [e(nc.sbuf_tensor(f"s_rs{i}", [128, 1], F32)) for i in range(4)]
        s_sh = [e(nc.sbuf_tensor(f"s_sh{i}", [128, 1], F32)) for i in range(4)]
        prodscr = vcur[0]

        pb = [e(nc.psum_tensor(f"pb{i}", [128, 512], F32)) for i in range(8)]
        bankfree = {i: None for i in range(8)}

        sems = {n: e(nc.semaphore(n)) for n in ("pe", "ve", "sc") + DMA_SEMS}

        def dma(dst, src, sem, wait=()):
            return p.emit("sync", lambda E, d=dst, s=src: E.dma_start(out=d, in_=s),
                          wait=wait, inc=sem)

        def gdma(dst, src, sem, wait=()):
            return p.emit("gpsimd", lambda E, d=dst, s=src: E.dma_start(out=d, in_=s),
                          wait=wait, inc=sem)

        def store(dst, src, sem="dbgst", wait=()):
            return p.emit("gpsimd", lambda E, d=dst, s=src: E.dma_start(out=d, in_=s),
                          wait=wait, inc=sem)

        def pe(fn, wait=()):
            return p.emit("tensor", fn, wait=wait, inc="pe")

        def ve(fn, wait=()):
            return p.emit("vector", fn, wait=wait, inc="ve")

        def sc(fn, wait=()):
            return p.emit("scalar", fn, wait=wait, inc="sc")

        def bw(i):
            return [bankfree[i]] if bankfree[i] else []

        dma(ident[:], ident_in, "const")
        dma(cpool[:], cpool_in, "const")
        for v in range(3):
            dma(masks[:, 320 * v:320 * (v + 1)], masks_in[v], "const")
        dma(xh_tm[:], xm[0:HALO, :], "const")
        dma(xg_tm[0:64, :], xg_in, "const")
        dma(fbias[:], bias_in, "const")
        T_CONST = p.ctr["const"]
        ve(lambda E: E.memset(ones1[:], 1.0))

        def fm_transpose_pro(src_tm, dst, n_tok, wait_extra):
            tv = None
            for g in range(2):
                bank = 6 + g
                tk = None
                for c in range(4 * g, 4 * g + 4):
                    tk = pe(lambda E, c=c, bank=bank: E.transpose(
                        pb[bank][:, n_tok * (c % 4):n_tok * (c % 4) + n_tok],
                        src_tm[0:n_tok, 128 * c:128 * c + 128],
                        ident[0:n_tok, 0:n_tok]),
                        wait=wait_extra + bw(bank))
                tv = ve(lambda E, g=g, bank=bank: E.tensor_copy(
                    dst[:, 4 * n_tok * g:4 * n_tok * (g + 1)],
                    pb[bank][:, 0:4 * n_tok]),
                    wait=[("pe", tk)])
                bankfree[bank] = ("ve", tv)
            return tv

        t_xhfm = fm_transpose_pro(xh_tm, xh_fm, 128, [("const", T_CONST)])
        t_xgfm = fm_transpose_pro(xg_tm, xg_fm, 64, [("const", T_CONST)])

        wcol_used = {}
        t_kp = None
        tk = None
        for k in range(NCH):
            buf = wcol[k % 2]
            wsem = f"wc{k % 2}"
            wb = [("pe", wcol_used[k - 2])] if k >= 2 else []
            tl = dma(buf[:], wqkv[128 * k:128 * (k + 1), 1024:2048], wsem, wait=wb)
            for m in range(8):
                tk = pe(lambda E, k=k, m=m, buf=buf: E.matmul(
                    pb[m][:, 0:128], buf[:, 128 * m:128 * m + 128],
                    xh_fm[:, 128 * k:128 * k + 128],
                    start=(k == 0), stop=False),
                    wait=[(wsem, tl), ("ve", t_xhfm)] + (bw(m) if k == 0 else []))
            for m in range(8):
                tk = pe(lambda E, k=k, m=m, buf=buf: E.matmul(
                    pb[m][:, 128:192], buf[:, 128 * m:128 * m + 128],
                    xg_fm[:, 64 * k:64 * k + 64],
                    start=False, stop=(k == NCH - 1)),
                    wait=[("ve", t_xgfm)])
            wcol_used[k] = tk
        for m in range(8):
            t_kp = ve(lambda E, m=m: E.tensor_copy(
                kwin[:, KW * m + 64:KW * m + 192], pb[m][:, 0:128]),
                wait=[("pe", tk)])
            t_kp = ve(lambda E, m=m: E.tensor_copy(
                kwin[:, KW * m:KW * m + 64], pb[m][:, 128:192]),
                wait=[("pe", tk)])
            bankfree[m] = ("ve", t_kp)
        wcol_prev_last2 = [wcol_used[6], wcol_used[7]]

        wrow_used = {}
        tkh = tkg2 = None
        for k in range(NCH):
            buf = wrow[k % 3]
            wsem = f"wr{k % 3}"
            wb = [("pe", wrow_used[k - 3])] if k >= 3 else []
            tl = dma(buf[:], wqkv[128 * k:128 * (k + 1), 2048:3072], wsem, wait=wb)
            for half in range(2):
                tkh = pe(lambda E, k=k, half=half, buf=buf: E.matmul(
                    pb[2 + half][:], xh_fm[:, 128 * k:128 * k + 128],
                    buf[:, 512 * half:512 * half + 512],
                    start=(k == 0), stop=(k == NCH - 1)),
                    wait=[(wsem, tl), ("ve", t_xhfm)] + (bw(2 + half) if k == 0 else []))
            for half in range(2):
                tkg2 = pe(lambda E, k=k, half=half, buf=buf: E.matmul(
                    pb[4 + half][0:64, :], xg_fm[:, 64 * k:64 * k + 64],
                    buf[:, 512 * half:512 * half + 512],
                    start=(k == 0), stop=(k == NCH - 1)),
                    wait=[("ve", t_xgfm)] + (bw(4 + half) if k == 0 else []))
            wrow_used[k] = tkg2
        wrow_prev_last3 = [wrow_used[6], wrow_used[7], wrow_used[5]]
        t_vp = t_vg = None
        for half in range(2):
            t_vp = ve(lambda E, half=half: E.tensor_copy(
                vprev[:, 512 * half:512 * half + 512], pb[2 + half][:]),
                wait=[("pe", tkh)])
            bankfree[2 + half] = ("ve", t_vp)
        for half in range(2):
            t_vg = ve(lambda E, half=half: E.tensor_copy(
                vglob[0:64, 512 * half:512 * half + 512], pb[4 + half][0:64, :]),
                wait=[("pe", tkg2)])
            bankfree[4 + half] = ("ve", t_vg)
        prev_tok = max(t_kp, t_vp)

        t_wfin = None
        t_wb_cons = [None, None]
        for dst, asrc, bsrc in ((wtr, aer_in, ber_in), (wti, aei_in, bei_in)):
            for k in range(4):
                for c in range(4):
                    wa = [("ve", t_wb_cons[0])] if t_wb_cons[0] else []
                    wb_ = [("ve", t_wb_cons[1])] if t_wb_cons[1] else []
                    ta = gdma(wbuf[0][:], asrc[c, 128 * k:128 * k + 128, :],
                              "wbA", wait=wa)
                    tb = gdma(wbuf[1][:], bsrc[c, 128 * k:128 * k + 128, :],
                              "wbB", wait=wb_)
                    if c == 0:
                        t_wfin = ve(lambda E, dst=dst, k=k: E.tensor_tensor(
                            dst[:, 512 * k:512 * (k + 1)],
                            wbuf[0][:], wbuf[1][:], ALU.mult),
                            wait=[("wbA", ta), ("wbB", tb)])
                    else:
                        t_wfin = ve(lambda E: E.tensor_tensor(
                            prodscr[:, 0:512], wbuf[0][:], wbuf[1][:], ALU.mult),
                            wait=[("wbA", ta), ("wbB", tb)])
                        t_wfin = ve(lambda E, dst=dst, k=k: E.tensor_tensor(
                            dst[:, 512 * k:512 * (k + 1)],
                            dst[:, 512 * k:512 * (k + 1)], prodscr[:, 0:512],
                            ALU.add))
                    t_wb_cons = [t_wfin, t_wfin]
        for k in range(4):
            t_wfin = ve(lambda E, k=k: E.tensor_scalar(
                wtn[:, 512 * k:512 * (k + 1)], wti[:, 512 * k:512 * (k + 1)],
                cpool[:, C_NEG1:C_NEG1 + 1], None, ALU.mult),
                wait=[("const", T_CONST)])

        t_xload = {}
        for sub in range(NSUB):
            wpro = [("ve", t_xgfm)] if sub < 2 else []
            t_xload[(0, sub)] = dma(x_tm[sub][:],
                                    xm[HALO + 128 * sub:HALO + 128 * (sub + 1), :],
                                    f"x{sub}", wait=wpro)

        if dbg:
            store(dbg_t["dbg_wtr"], wtr[:], wait=[("ve", t_wfin)])
            store(dbg_t["dbg_wti"], wti[:], wait=[("ve", t_wfin)])
            store(dbg_t["dbg_vglob"], vglob[0:64, :], wait=[("ve", t_vg)])
            store(dbg_t["dbg_vprev"], vprev[:], wait=[("ve", t_vp)])

        t_sqscr_rd = None
        t_attn_done_prev = None
        t_proj_done_prev = None
        t_ffn_done_prev = None
        t_resid = {}
        t_store = {}
        t_ln1nrm_all = {}
        t_ln2nrm_all = {}
        wout_prev_last3 = wrow_prev_last3

        for t in range(nt):
            xb = [x_tm[s] for s in range(NSUB)]

            w_ffnprev = [("pe", t_ffn_done_prev)] if t_ffn_done_prev else []
            t_xT = None
            for sub in range(NSUB):
                for g in range(2):
                    bank = 6 + g
                    tk = None
                    for c in range(4 * g, 4 * g + 4):
                        tk = pe(lambda E, c=c, bank=bank, x=xb[sub]: E.transpose(
                            pb[bank][:, 128 * (c % 4):128 * (c % 4) + 128],
                            x[:, 128 * c:128 * c + 128], ident[:]),
                            wait=[(f"x{sub}", t_xload[(t, sub)])] + bw(bank))
                    for c in range(4 * g, 4 * g + 4):
                        t_xT = ve(lambda E, c=c, sub=sub, bank=bank: E.tensor_copy(
                            xT[:, 512 * c + 128 * sub:512 * c + 128 * sub + 128],
                            pb[bank][:, 128 * (c % 4):128 * (c % 4) + 128]),
                            wait=[("pe", tk)] + w_ffnprev)
                    bankfree[bank] = ("ve", t_xT)

            w_attnprev = [("pe", t_attn_done_prev)] if t_attn_done_prev else []
            t_q_tok = {}
            t_k_tok = {}
            for part in range(2):
                tk = None
                for k in range(NCH):
                    buf = wcol[k % 2]
                    wsem = f"wc{k % 2}"
                    li = 2 * part + (k % 2)
                    wb = [("pe", wcol_used[k - 2])] if k >= 2 else \
                         [("pe", wcol_prev_last2[k])]
                    tl = dma(buf[:],
                             wqkv[128 * k:128 * (k + 1),
                                  1024 * part:1024 * part + 1024], wsem, wait=wb)
                    for m in range(8):
                        tk = pe(lambda E, k=k, m=m, buf=buf: E.matmul(
                            pb[m][:], buf[:, 128 * m:128 * m + 128],
                            xT[:, 512 * k:512 * (k + 1)],
                            start=(k == 0), stop=(k == NCH - 1)),
                            wait=[(wsem, tl), ("ve", t_xT)] +
                                 (bw(m) if k == 0 else []))
                    wcol_used[k] = tk
                wcol_prev_last2 = [wcol_used[6], wcol_used[7]]
                if part == 0:
                    for m in range(8):
                        tq = sc(lambda E, m=m: E.activation(
                            qfm[:, 512 * m:512 * (m + 1)], pb[m][:],
                            AF.Copy, scale=0.125),
                            wait=[("pe", tk)] + w_attnprev)
                        t_q_tok[m] = tq
                        bankfree[m] = ("sc", tq)
                else:
                    for m in range(8):
                        tkk = ve(lambda E, m=m: E.tensor_copy(
                            kwin[:, KW * m + 192:KW * m + KW], pb[m][:]),
                            wait=[("pe", tk), ("ve", prev_tok)] + w_attnprev)
                        t_k_tok[m] = tkk
                        bankfree[m] = ("ve", tkk)

            t_v_tok = {}
            for k in range(NCH):
                buf = wrow[k % 3]
                wsem = f"wr{k % 3}"
                if k >= 3:
                    wb = [("pe", wrow_used[k - 3])]
                elif t == 0:
                    wb = [("pe", wrow_prev_last3[k])]
                else:
                    wb = [("pe", wout_prev_last3[k])]
                tl = dma(buf[:], wqkv[128 * k:128 * (k + 1), 2048:3072], wsem, wait=wb)
                tk = None
                for s in range(NSUB):
                    for half in range(2):
                        bank = 2 * s + half
                        tk = pe(lambda E, k=k, s=s, half=half, buf=buf, bank=bank:
                                E.matmul(
                                    pb[bank][:],
                                    xT[:, 512 * k + 128 * s:512 * k + 128 * s + 128],
                                    buf[:, 512 * half:512 * half + 512],
                                    start=(k == 0), stop=(k == NCH - 1)),
                                wait=[(wsem, tl), ("ve", t_xT)] +
                                     (bw(bank) if k == 0 else []))
                wrow_used[k] = tk
            for s in range(NSUB):
                for half in range(2):
                    bank = 2 * s + half
                    tvv = ve(lambda E, s=s, half=half, bank=bank: E.tensor_copy(
                        vcur[s][:, 512 * half:512 * half + 512], pb[bank][:]),
                        wait=[("pe", wrow_used[NCH - 1]), ("ve", prev_tok)] + w_attnprev)
                    t_v_tok[s] = tvv
                    bankfree[bank] = ("ve", tvv)

            w_projprev = [("pe", t_proj_done_prev)] if t_proj_done_prev else []
            seq = [(pair, r, hh) for pair in range(8) for r in range(NSUB)
                   for hh in range(2)]
            NIT = len(seq)
            DEPTH = 2
            tok_exp = {}
            tok_nm = {}
            tok_trans = {}
            tok_tpe = {}
            tok_av = {}
            t_ao_tok = {}
            t_attn_done = None

            def emit_head(i):
                pair, r, hh = seq[i]
                midx = min(4 * t + r, 2)
                i3 = i % 3
                sb = i % 3
                q_ap = qfm[64 * hh:64 * hh + 64,
                           512 * pair + 128 * r:512 * pair + 128 * r + 128]
                wsc = [("sc", t_q_tok[pair]), ("ve", t_k_tok[pair]),
                       ("ve", prev_tok)] + bw(sb)
                if r == 0:
                    tk1 = pe(lambda E, q_ap=q_ap, hh=hh, pair=pair, sb=sb: E.matmul(
                        pb[sb][:, 0:320], q_ap,
                        kwin[64 * hh:64 * hh + 64, KW * pair:KW * pair + 320],
                        start=True, stop=True), wait=wsc)
                else:
                    tk1 = pe(lambda E, q_ap=q_ap, hh=hh, pair=pair, sb=sb: E.matmul(
                        pb[sb][:, 0:64], q_ap,
                        kwin[64 * hh:64 * hh + 64, KW * pair:KW * pair + 64],
                        start=True, stop=True), wait=wsc)
                    tk1 = pe(lambda E, q_ap=q_ap, hh=hh, pair=pair, r=r, sb=sb:
                             E.matmul(
                                 pb[sb][:, 64:320], q_ap,
                                 kwin[64 * hh:64 * hh + 64,
                                      KW * pair + 64 + 128 * r:
                                      KW * pair + 64 + 128 * r + 256],
                                 start=True, stop=True))
                tmx = ve(lambda E, sb=sb, i3=i3: E.reduce_max(
                    smax[i3][:], pb[sb][:, 0:320], axis=AX.X, negate=True),
                    wait=[("pe", tk1)] +
                         ([("sc", tok_exp[i - 3])] if i - 3 in tok_exp else []))
                tex = sc(lambda E, sb=sb, i3=i3: E.activation(
                    pexp[i3][:], pb[sb][:, 0:320], AF.Exp,
                    bias=smax[i3][:], scale=1.0),
                    wait=[("ve", tmx)] +
                         ([("pe", tok_trans[i - 3])] if i - 3 in tok_trans else []))
                tok_exp[i] = tex
                bankfree[sb] = ("sc", tex)
                tms = ve(lambda E, i3=i3, midx=midx: E.scalar_tensor_tensor(
                    pexp[i3][:], pexp[i3][:], cpool[:, C_ONE:C_ONE + 1],
                    masks[:, 320 * midx:320 * (midx + 1)],
                    ALU.mult, ALU.mult, accum_out=ssum[i3][:]),
                    wait=[("sc", tex), ("const", T_CONST)])
                trec = ve(lambda E, i3=i3: E.reciprocal(
                    srinv[i3][:], ssum[i3][:]), wait=[("ve", tms)])
                tnm = ve(lambda E, i3=i3: E.tensor_scalar(
                    pexp[i3][:], pexp[i3][:], srinv[i3][:], None, ALU.mult),
                    wait=[("ve", trec)])
                tok_nm[i] = tnm

            def emit_trans(j):
                pair, r, hh = seq[j]
                j3 = j % 3
                tb = 3 if j % 2 == 0 else 6
                tk2 = pe(lambda E, j3=j3, tb=tb: E.transpose(
                    pb[tb][0:64, 0:128], pexp[j3][:, 0:64], ident[:]),
                    wait=[("ve", tok_nm[j])] + bw(tb))
                tk2 = pe(lambda E, j3=j3, tb=tb: E.transpose(
                    pb[tb][:, 128:256], pexp[j3][:, 64:192], ident[:]))
                tk2 = pe(lambda E, j3=j3, tb=tb: E.transpose(
                    pb[tb][:, 256:384], pexp[j3][:, 192:320], ident[:]))
                tok_trans[j] = tk2
                tpe = sc(lambda E, j3=j3, j2=j % 2, tb=tb: E.copy(
                    pT_sb[j2][:], pb[tb][:, 0:384]),
                    wait=[("pe", tk2)] +
                         ([("pe", tok_av[j - 2])] if j - 2 in tok_av else []))
                tok_tpe[j] = tpe
                bankfree[tb] = ("sc", tpe)

            def emit_tail(j):
                nonlocal t_attn_done
                pair, r, hh = seq[j]
                h = 2 * pair + hh
                av_bank = 4 + (pair % 2)
                tpe = tok_tpe[j]
                tp_av = (0, 64) if hh else None
                wav = [("sc", tpe), ("ve", t_v_tok[r]), ("ve", t_vg),
                       ("ve", t_vp)] + \
                      ([("ve", t_v_tok[r - 1])] if r >= 1 else []) + \
                      (bw(av_bank) if (r == 0 and hh == 0) else [])
                vband = vprev if r == 0 else vcur[r - 1]
                j2 = j % 2
                tk3 = pe(lambda E, h=h, hh=hh, r=r, j2=j2, ab=av_bank,
                         tp=tp_av: E.matmul(
                             pb[ab][64 * hh:64 * hh + 64, 128 * r:128 * r + 128],
                             vglob[0:64, 64 * h:64 * h + 64],
                             pT_sb[j2][0:64, 0:128],
                             start=True, stop=False, tile_position=tp),
                         wait=wav)
                tk3 = pe(lambda E, h=h, hh=hh, r=r, j2=j2, ab=av_bank,
                         vb=vband, tp=tp_av: E.matmul(
                             pb[ab][64 * hh:64 * hh + 64, 128 * r:128 * r + 128],
                             vb[:, 64 * h:64 * h + 64],
                             pT_sb[j2][:, 128:256],
                             start=False, stop=False, tile_position=tp))
                tk3 = pe(lambda E, h=h, hh=hh, r=r, j2=j2, ab=av_bank,
                         tp=tp_av: E.matmul(
                             pb[ab][64 * hh:64 * hh + 64, 128 * r:128 * r + 128],
                             vcur[r][:, 64 * h:64 * h + 64],
                             pT_sb[j2][:, 256:384],
                             start=False, stop=True, tile_position=tp))
                tok_av[j] = tk3
                t_attn_done = tk3
                if r == NSUB - 1 and hh == 1:
                    tao = ve(lambda E, pair=pair, ab=av_bank: E.tensor_copy(
                        aoT[:, 512 * pair:512 * (pair + 1)], pb[ab][:]),
                        wait=[("pe", tk3)] + w_projprev)
                    t_ao_tok[pair] = tao
                    bankfree[av_bank] = ("ve", tao)

            for step in range(NIT + 3):
                if step < NIT:
                    emit_head(step)
                if DEPTH <= step < NIT + DEPTH:
                    emit_trans(step - DEPTH)
                if step >= 3:
                    emit_tail(step - 3)

            if t + 1 < nt:
                tpc = None
                for c in range(NCH):
                    tpc = ve(lambda E, c=c: E.tensor_copy(
                        kwin[:, KW * c + 64:KW * c + 192],
                        kwin[:, KW * c + 192 + 384:KW * c + KW]),
                        wait=[("pe", t_attn_done)])
                tpv = ve(lambda E: E.tensor_copy(vprev[:], vcur[3][:]),
                         wait=[("pe", t_attn_done)])
                prev_tok = tpv

            t_wout_load = {}
            t_wout_used = {}
            t_proj_bank = {}
            for k in range(NCH):
                buf = wrow[k % 3]
                wsem = f"wr{k % 3}"
                if k >= 3:
                    wb = [("pe", t_wout_used[k - 3])]
                else:
                    wb = [("pe", wrow_used[{0: 6, 1: 7, 2: 5}[k]])]
                tl = dma(buf[:], wout[128 * k:128 * (k + 1), :], wsem, wait=wb)
                t_wout_load[k] = tl
                tk = None
                for r in range(NSUB):
                    for half in range(2):
                        bank = 2 * r + half
                        tk = pe(lambda E, k=k, r=r, half=half, buf=buf, bank=bank:
                                E.matmul(
                                    pb[bank][:],
                                    aoT[:, 512 * k + 128 * r:512 * k + 128 * r + 128],
                                    buf[:, 512 * half:512 * half + 512],
                                    start=(k == 0), stop=(k == NCH - 1)),
                                wait=[(wsem, tl), ("ve", t_ao_tok[k])] +
                                     (bw(bank) if k == 0 else []))
                        if k == NCH - 1:
                            t_proj_bank[(r, half)] = tk
                t_wout_used[k] = tk
            wout_prev_last3 = [t_wout_used[6], t_wout_used[7], t_wout_used[5]]
            t_proj_done = t_wout_used[NCH - 1]
            t_ln1pre = {}
            for r in range(NSUB):
                tres = None
                for half in range(2):
                    bank = 2 * r + half
                    tres = ve(lambda E, r=r, half=half, bank=bank, x=xb[r]:
                              E.scalar_tensor_tensor(
                                  ln1tm[r][:, 512 * half:512 * half + 512],
                                  pb[bank][:], cpool[:, C_ONE:C_ONE + 1],
                                  x[:, 512 * half:512 * half + 512],
                                  ALU.mult, ALU.add),
                              wait=[("pe", t_proj_bank[(r, half)])])
                    bankfree[bank] = ("ve", tres)
                t_ln1pre[r] = tres
            t_resid[t] = t_ln1pre[NSUB - 1]
            if t + 1 < nt:
                for sub in range(NSUB):
                    t_xload[(t + 1, sub)] = dma(
                        x_tm[sub][:],
                        xm[HALO + 512 * (t + 1) + 128 * sub:
                           HALO + 512 * (t + 1) + 128 * (sub + 1), :],
                        f"x{sub}", wait=[("ve", t_ln1pre[sub])])

            def ln_stats(si_, inp_fn, pre_wait, wpar):
                nonlocal t_sqscr_rd
                tsqr = sc(lambda E: E.activation(
                    sqscr[:], inp_fn(), AF.Square),
                    wait=pre_wait + wpar +
                         ([("ve", t_sqscr_rd)] if t_sqscr_rd else []))
                tsq = ve(lambda E, si_=si_: E.reduce_sum(
                    s_sq[si_][:], sqscr[:], axis=AX.X), wait=[("sc", tsqr)])
                t_sqscr_rd = tsq
                tred = ve(lambda E, si_=si_: E.reduce_sum(
                    s_sum[si_][:], inp_fn(), axis=AX.X))
                tt = ve(lambda E, si_=si_: E.tensor_scalar(
                    s_mean[si_][:], s_sum[si_][:], cpool[:, C_INV1024:C_INV1024 + 1],
                    None, ALU.mult), wait=[("const", T_CONST), ("ve", tred)])
                tt = ve(lambda E, si_=si_: E.tensor_scalar(
                    s_msq[si_][:], s_sq[si_][:], cpool[:, C_INV1024:C_INV1024 + 1],
                    None, ALU.mult), wait=[("ve", tsq), ("ve", tt)])
                tt = ve(lambda E, si_=si_: E.tensor_tensor(
                    s_var[si_][:], s_mean[si_][:], s_mean[si_][:], ALU.mult),
                    wait=[("ve", tt)])
                tt = ve(lambda E, si_=si_: E.tensor_tensor(
                    s_var[si_][:], s_msq[si_][:], s_var[si_][:], ALU.subtract),
                    wait=[("ve", tt)])
                tvar = ve(lambda E, si_=si_: E.tensor_scalar(
                    s_var[si_][:], s_var[si_][:], cpool[:, C_EPS:C_EPS + 1],
                    None, ALU.add), wait=[("ve", tt)])
                tsd = sc(lambda E, si_=si_: E.activation(
                    s_sd[si_][:], s_var[si_][:], AF.Sqrt), wait=[("ve", tvar)])
                trc = ve(lambda E, si_=si_: E.reciprocal(
                    s_rs[si_][:], s_sd[si_][:]), wait=[("sc", tsd)])
                tsh = ve(lambda E, si_=si_: E.scalar_tensor_tensor(
                    s_sh[si_][:], s_mean[si_][:], cpool[:, C_NEG1:C_NEG1 + 1],
                    s_rs[si_][:], ALU.mult, ALU.mult), wait=[("ve", trc)])
                return tsh

            t_ln1nrm = {}
            t_ln2nrm = {}
            t_ffn_done = None
            for r in range(NSUB):
                si_ = r % 2
                if (r - 2) in t_ln1nrm:
                    wpar = [("ve", t_ln1nrm[r - 2])]
                elif t > 0:
                    wpar = [("ve", t_ln1nrm_all[t - 1][r + 2])]
                else:
                    wpar = []
                tsh = ln_stats(si_, (lambda r=r: ln1tm[r][:]),
                               [("ve", t_ln1pre[r])], wpar)
                tnm = ve(lambda E, r=r, si_=si_: E.tensor_scalar(
                    ln1tm[r][:], ln1tm[r][:], s_rs[si_][:], s_sh[si_][:],
                    ALU.mult, ALU.add), wait=[("ve", tsh)])
                t_ln1nrm[r] = tnm
            for r in range(NSUB):
                tnm = t_ln1nrm[r]
                t_ln1fm_r = None
                for g in range(2):
                    bank = 2 + g
                    tk = None
                    for c in range(4 * g, 4 * g + 4):
                        tk = pe(lambda E, c=c, r=r, bank=bank: E.transpose(
                            pb[bank][:, 128 * (c % 4):128 * (c % 4) + 128],
                            ln1tm[r][:, 128 * c:128 * c + 128], ident[:]),
                            wait=[("ve", tnm)] + bw(bank))
                    for c in range(4 * g, 4 * g + 4):
                        t_ln1fm_r = ve(lambda E, c=c, r=r, bank=bank: E.tensor_copy(
                            xT[:, 512 * c + 128 * r:512 * c + 128 * r + 128],
                            pb[bank][:, 128 * (c % 4):128 * (c % 4) + 128]),
                            wait=[("pe", tk)])
                    bankfree[bank] = ("ve", t_ln1fm_r)
                banks = (4, 5) if r % 2 == 0 else (6, 7)
                tk = None
                for c in range(4):
                    lhs = xT[:, 512 * c + 128 * r:512 * c + 128 * r + 128]
                    tk = pe(lambda E, lhs=lhs, c=c, bank=banks[0]: E.matmul(
                        pb[bank][:], lhs, wtr[:, 512 * c:512 * (c + 1)],
                        start=(c == 0), stop=False),
                        wait=[("ve", t_ln1fm_r), ("ve", t_wfin)] +
                             (bw(banks[0]) if c == 0 else []))
                    tk = pe(lambda E, lhs=lhs, c=c, bank=banks[1]: E.matmul(
                        pb[bank][:], lhs, wti[:, 512 * c:512 * (c + 1)],
                        start=(c == 0), stop=False),
                        wait=(bw(banks[1]) if c == 0 else []))
                for c in range(4):
                    lhs = xT[:, 512 * (4 + c) + 128 * r:512 * (4 + c) + 128 * r + 128]
                    tk = pe(lambda E, lhs=lhs, c=c, bank=banks[0]: E.matmul(
                        pb[bank][:], lhs, wtn[:, 512 * c:512 * (c + 1)],
                        start=False, stop=False))
                    tk = pe(lambda E, lhs=lhs, c=c, bank=banks[1]: E.matmul(
                        pb[bank][:], lhs, wtr[:, 512 * c:512 * (c + 1)],
                        start=False, stop=False))
                tk = pe(lambda E, bank=banks[0]: E.matmul(
                    pb[bank][:], ones1[:], fbias[:, 0:512],
                    start=False, stop=True), wait=[("const", T_CONST)])
                tk = pe(lambda E, bank=banks[1]: E.matmul(
                    pb[bank][:], ones1[:], fbias[:, 512:1024],
                    start=False, stop=True))
                t_ffn_done = tk
                ob = otm[r % 2]
                stsem = f"st{r % 2}"
                if (t, r - 2) in t_store:
                    wst = [(stsem, t_store[(t, r - 2)])]
                elif (t - 1, r + 2) in t_store:
                    wst = [(stsem, t_store[(t - 1, r + 2)])]
                else:
                    wst = []
                tpre = None
                for half in range(2):
                    bank = banks[half]
                    tpre = ve(lambda E, r=r, half=half, bank=bank, ob=ob:
                              E.scalar_tensor_tensor(
                                  ob[:, 512 * half:512 * half + 512],
                                  pb[bank][:], cpool[:, C_ONE:C_ONE + 1],
                                  ln1tm[r][:, 512 * half:512 * half + 512],
                                  ALU.mult, ALU.add),
                              wait=[("pe", tk)] + wst)
                    bankfree[bank] = ("ve", tpre)
                si2 = 2 + r % 2
                if (r - 2) in t_ln2nrm:
                    wpar = [("ve", t_ln2nrm[r - 2])]
                elif t > 0:
                    wpar = [("ve", t_ln2nrm_all[t - 1][r + 2])]
                else:
                    wpar = []
                tsh = ln_stats(si2, (lambda ob=ob: ob[:]), [("ve", tpre)], wpar)
                tnm = ve(lambda E, ob=ob, si2=si2: E.tensor_scalar(
                    ob[:], ob[:], s_rs[si2][:], s_sh[si2][:],
                    ALU.mult, ALU.add), wait=[("ve", tsh)])
                t_ln2nrm[r] = tnm
                t_store[(t, r)] = store(
                    y[512 * t + 128 * r:512 * t + 128 * (r + 1), :], ob[:],
                    sem=stsem, wait=[("ve", tnm)])
            t_ln1nrm_all[t] = t_ln1nrm
            t_ln2nrm_all[t] = t_ln2nrm

            if dbg and t == 0:
                store(dbg_t["dbg_qfm"], qfm[:], wait=[("sc", t_q_tok[7])])
                store(dbg_t["dbg_kwin"], kwin[:], wait=[("ve", t_k_tok[7])])
                for s in range(NSUB):
                    store(dbg_t["dbg_vcur"][s], vcur[s][:], wait=[("ve", t_v_tok[s])])
                store(dbg_t["dbg_aot"], aoT[:], wait=[("ve", t_ao_tok[7])])
                for r in range(NSUB):
                    store(dbg_t["dbg_ln1"][r], ln1tm[r][:],
                          wait=[("ve", t_ln1nrm[r])])
                store(dbg_t["dbg_xt"], xT[:], wait=[("ve", t_ln1fm_r)])

            t_attn_done_prev = t_attn_done
            t_proj_done_prev = t_proj_done
            t_ffn_done_prev = t_ffn_done

        p.emit("gpsimd", None,
               wait=[(s, p.ctr[s]) for s in ("st0", "st1", "dbgst")
                     if p.ctr[s] > 0])

        globals()["_last_plan"] = dict(p.ctr)

        with nc.Block() as block:
            def mk(eng_name):
                def body(E):
                    for waits, fn, inc, amt in p.ops[eng_name]:
                        for s, v in waits:
                            E.wait_ge(sems[s], v)
                        if fn is not None:
                            inst = fn(E)
                            if inc:
                                inst.then_inc(sems[inc], amt)
                return body
            block.sync(mk("sync"))
            block.tensor(mk("tensor"))
            block.vector(mk("vector"))
            block.scalar(mk("scalar"))
            block.gpsimd(mk("gpsimd"))

    return nc



def make_core_inputs(x, w_qkv, w_out, G1_R, G2_R, G1_I, G2_I, b_r, b_i):
    x = np.ascontiguousarray(np.asarray(x, dtype=np.float32))
    w_qkv = np.ascontiguousarray(np.asarray(w_qkv, dtype=np.float32))
    w_out = np.ascontiguousarray(np.asarray(w_out, dtype=np.float32))
    r_idx = np.arange(128)[:, None]
    c_idx = np.arange(128)[None, :]
    low = (c_idx <= r_idx).astype(np.float32)
    up = (c_idx >= r_idx).astype(np.float32)
    up64 = ((c_idx >= r_idx) | (c_idx < 64)).astype(np.float32)
    g0 = np.zeros((128, 64), np.float32)
    g1 = np.ones((128, 64), np.float32)
    z128 = np.zeros((128, 128), np.float32)
    masks_h0 = np.ascontiguousarray(np.stack([
        np.concatenate([g0, z128, low], axis=1),
        np.concatenate([g0, up64, low], axis=1),
        np.concatenate([g1, up, low], axis=1)]))
    m2 = np.concatenate([g1, up, low], axis=1)
    masks_h1 = np.ascontiguousarray(np.stack([m2, m2, m2]))
    ident = np.eye(128, dtype=np.float32)
    cpool = np.zeros((128, 8), np.float32)
    cpool[:, C_INV1024] = 1.0 / 1024.0
    cpool[:, C_EPS] = 1e-5
    cpool[:, C_NEG1] = -1.0
    cpool[:, C_ONE] = 1.0

    oi = np.arange(512)
    aexp = {}
    bexp = {}
    for tag, G1, G2 in (("r", G1_R, G2_R), ("i", G1_I, G2_I)):
        G1 = np.asarray(G1, np.float32)
        G2 = np.asarray(G2, np.float32)
        a = np.empty((4, 512, 512), np.float32)
        b = np.empty((4, 512, 512), np.float32)
        for c in range(4):
            A = G1[0, :, c, :]
            Bm = G2[c, :, 0, :]
            a[c] = A[oi // 16, :][:, oi // 16].T
            b[c] = Bm[oi % 16, :][:, oi % 16].T
        aexp[tag] = np.ascontiguousarray(a)
        bexp[tag] = np.ascontiguousarray(b)
    fbias = np.ascontiguousarray(
        np.concatenate([np.asarray(b_r, np.float32),
                        np.asarray(b_i, np.float32)])[None, :])

    in_maps = []
    for core in range(8):
        b = core // 2
        half = core % 2
        if half == 0:
            x_main = np.concatenate([np.zeros((HALO, D), np.float32),
                                     x[b, 0:OWN]], axis=0)
        else:
            x_main = x[b, OWN - HALO:SEQ]
        in_maps.append({
            "x_main": np.ascontiguousarray(x_main),
            "x_glob": np.ascontiguousarray(x[b, 0:GLOB]),
            "w_qkv": w_qkv,
            "w_out": w_out,
            "masks": masks_h0 if half == 0 else masks_h1,
            "ident": ident,
            "cpool": cpool,
            "aexp_r": aexp["r"], "bexp_r": bexp["r"],
            "aexp_i": aexp["i"], "bexp_i": bexp["i"],
            "fbias": fbias,
        })
    return in_maps


def kernel(L0_attn_x, L0_attn_w_qkv, L0_attn_w_out, L0_ffn_G1_R, L0_ffn_G2_R,
           L0_ffn_G1_I, L0_ffn_G2_I, L0_ffn_bias_real, L0_ffn_bias_imag):
    in_maps = make_core_inputs(L0_attn_x, L0_attn_w_qkv, L0_attn_w_out,
                               L0_ffn_G1_R, L0_ffn_G2_R, L0_ffn_G1_I,
                               L0_ffn_G2_I, L0_ffn_bias_real, L0_ffn_bias_imag)
    nc = build_program(dbg=False, nt=NT)
    res = run_bass_kernel_spmd(nc, in_maps, core_ids=list(range(8)))
    out = np.empty((NB, SEQ, D), dtype=np.float32)
    for core in range(8):
        b, half = core // 2, core % 2
        out[b, OWN * half:OWN * (half + 1)] = res.results[core]["y"]
    return out


# version 04 (final)
# speedup vs baseline: 1.3748x; 1.0678x over previous
import numpy as np
from contextlib import ExitStack
import concourse.bass as bass
import concourse.mybir as mybir
from concourse.bass_utils import run_bass_kernel_spmd

F32 = mybir.dt.float32
AF = mybir.ActivationFunctionType
ALU = mybir.AluOpType
AX = mybir.AxisListType

D = 1024
SEQ = 4096
NB = 4
NSUB = 4
NT = 4
OWN = 2048
HALO = 128
GLOB = 64
NCH = 8
KW = 704

C_INV1024 = 0
C_EPS = 1
C_NEG1 = 2
C_ONE = 3

ENGS = ("sync", "tensor", "vector", "scalar", "gpsimd")

DMA_SEMS = ("const", "x0", "x1", "x2", "x3", "wc0", "wc1",
            "wr0", "wr1", "wr2", "wbA", "wbB", "st0", "st1", "dbgst")


class Plan:
    def __init__(self):
        self.ops = {e: [] for e in ENGS}
        self.ctr = {n: 0 for n in ("pe", "ve", "sc", "gp") + DMA_SEMS}
        self.lastw = {e: {} for e in ENGS}

    def emit(self, eng, fn, wait=(), inc=None):
        w = []
        for sem, val in wait:
            if not val:
                continue
            if val > self.lastw[eng].get(sem, 0):
                w.append((sem, val))
                self.lastw[eng][sem] = val
        amt = 0
        if inc:
            amt = 16 if inc in DMA_SEMS else 1
            self.ctr[inc] += amt
        self.ops[eng].append((tuple(w), fn, inc, amt))
        return self.ctr[inc] if inc else None


def build_program(dbg=False, nt=NT):
    nc = bass.Bass("TRN2", target_bir_lowering=False)

    xm = nc.dram_tensor("x_main", [HALO + OWN, D], F32, kind="ExternalInput").ap()
    xg_in = nc.dram_tensor("x_glob", [GLOB, D], F32, kind="ExternalInput").ap()
    wqkv = nc.dram_tensor("w_qkv", [D, 3 * D], F32, kind="ExternalInput").ap()
    wout = nc.dram_tensor("w_out", [D, D], F32, kind="ExternalInput").ap()
    masks_in = nc.dram_tensor("masks", [3, 128, 320], F32, kind="ExternalInput").ap()
    ident_in = nc.dram_tensor("ident", [128, 128], F32, kind="ExternalInput").ap()
    cpool_in = nc.dram_tensor("cpool", [128, 8], F32, kind="ExternalInput").ap()
    aer_in = nc.dram_tensor("aexp_r", [4, 512, 512], F32, kind="ExternalInput").ap()
    ber_in = nc.dram_tensor("bexp_r", [4, 512, 512], F32, kind="ExternalInput").ap()
    aei_in = nc.dram_tensor("aexp_i", [4, 512, 512], F32, kind="ExternalInput").ap()
    bei_in = nc.dram_tensor("bexp_i", [4, 512, 512], F32, kind="ExternalInput").ap()
    bias_in = nc.dram_tensor("fbias", [1, 1024], F32, kind="ExternalInput").ap()
    y = nc.dram_tensor("y", [OWN, D], F32, kind="ExternalOutput").ap()
    dbg_t = {}
    if dbg:
        for name, shape in (("dbg_qfm", [128, 4096]), ("dbg_kwin", [128, 8 * KW]),
                            ("dbg_vcur", [4, 128, 1024]),
                            ("dbg_vglob", [64, 1024]), ("dbg_vprev", [128, 1024]),
                            ("dbg_aot", [128, 4096]), ("dbg_ln1", [4, 128, 1024]),
                            ("dbg_wtr", [128, 2048]), ("dbg_wti", [128, 2048]),
                            ("dbg_xt", [128, 4096])):
            dbg_t[name] = nc.dram_tensor(name, shape, F32, kind="ExternalOutput").ap()

    p = Plan()

    with ExitStack() as ctx:
        e = ctx.enter_context
        ident = e(nc.sbuf_tensor("ident_sb", [128, 128], F32))
        cpool = e(nc.sbuf_tensor("cpool_sb", [128, 8], F32))
        masks = e(nc.sbuf_tensor("masks_sb", [128, 3 * 320], F32))
        x_tm = [e(nc.sbuf_tensor(f"x_tm{i}", [128, D], F32)) for i in range(4)]
        xh_tm = x_tm[0]
        xg_tm = x_tm[1]
        xh_fm = e(nc.sbuf_tensor("xh_fm", [128, 1024], F32))
        xg_fm = e(nc.sbuf_tensor("xg_fm", [128, 512], F32))
        xT = e(nc.sbuf_tensor("xT", [128, 4096], F32))
        qfm = e(nc.sbuf_tensor("qfm", [128, 4096], F32))
        kwin = e(nc.sbuf_tensor("kwin", [128, 8 * KW], F32))
        vcur = [e(nc.sbuf_tensor(f"vcur{s}", [128, D], F32)) for s in range(4)]
        vprev = e(nc.sbuf_tensor("vprev", [128, D], F32))
        vglob = e(nc.sbuf_tensor("vglob", [64, D], F32))
        aoT = e(nc.sbuf_tensor("aoT", [128, 4096], F32))
        ln1tm = [e(nc.sbuf_tensor(f"ln1tm{s}", [128, D], F32)) for s in range(4)]
        otm = [e(nc.sbuf_tensor(f"otm{i}", [128, D], F32)) for i in range(2)]
        sqscr = xh_fm
        pexp = [e(nc.sbuf_tensor(f"pexp{i}", [128, 320], F32)) for i in range(3)]
        pT_sb = [e(nc.sbuf_tensor(f"pT{i}", [128, 384], F32)) for i in range(2)]
        wcol = [e(nc.sbuf_tensor(f"wcol{i}", [128, 1024], F32)) for i in range(2)]
        wrow = [e(nc.sbuf_tensor(f"wrow{i}", [128, 1024], F32)) for i in range(3)]
        wbuf = [e(nc.sbuf_tensor(f"wbuf{i}", [128, 512], F32)) for i in range(2)]
        wtr = e(nc.sbuf_tensor("wtr", [128, 2048], F32))
        wti = e(nc.sbuf_tensor("wti", [128, 2048], F32))
        wtn = e(nc.sbuf_tensor("wtn", [128, 2048], F32))
        fbias = e(nc.sbuf_tensor("fbias_sb", [1, 1024], F32))
        ones1 = e(nc.sbuf_tensor("ones1", [1, 128], F32))
        smax = [e(nc.sbuf_tensor(f"smax{i}", [128, 1], F32)) for i in range(3)]
        ssum = [e(nc.sbuf_tensor(f"ssum{i}", [128, 1], F32)) for i in range(3)]
        srinv = [e(nc.sbuf_tensor(f"srinv{i}", [128, 1], F32)) for i in range(3)]
        s_sum = [e(nc.sbuf_tensor(f"s_sum{i}", [128, 1], F32)) for i in range(4)]
        s_sq = [e(nc.sbuf_tensor(f"s_sq{i}", [128, 1], F32)) for i in range(4)]
        s_mean = [e(nc.sbuf_tensor(f"s_mean{i}", [128, 1], F32)) for i in range(4)]
        s_msq = [e(nc.sbuf_tensor(f"s_msq{i}", [128, 1], F32)) for i in range(4)]
        s_var = [e(nc.sbuf_tensor(f"s_var{i}", [128, 1], F32)) for i in range(4)]
        s_sd = [e(nc.sbuf_tensor(f"s_sd{i}", [128, 1], F32)) for i in range(4)]
        s_rs = [e(nc.sbuf_tensor(f"s_rs{i}", [128, 1], F32)) for i in range(4)]
        s_sh = [e(nc.sbuf_tensor(f"s_sh{i}", [128, 1], F32)) for i in range(4)]
        prodscr = vcur[0]

        pb = [e(nc.psum_tensor(f"pb{i}", [128, 512], F32)) for i in range(8)]
        bankfree = {i: None for i in range(8)}

        sems = {n: e(nc.semaphore(n)) for n in ("pe", "ve", "sc", "gp") + DMA_SEMS}

        def dma(dst, src, sem, wait=()):
            return p.emit("sync", lambda E, d=dst, s=src: E.dma_start(out=d, in_=s),
                          wait=wait, inc=sem)

        def gdma(dst, src, sem, wait=()):
            return p.emit("gpsimd", lambda E, d=dst, s=src: E.dma_start(out=d, in_=s),
                          wait=wait, inc=sem)

        def store(dst, src, sem="dbgst", wait=()):
            return p.emit("gpsimd", lambda E, d=dst, s=src: E.dma_start(out=d, in_=s),
                          wait=wait, inc=sem)

        def pe(fn, wait=()):
            return p.emit("tensor", fn, wait=wait, inc="pe")

        def gve(fn, wait=()):
            return p.emit("gpsimd", fn, wait=wait, inc="gp")

        def ve(fn, wait=()):
            return p.emit("vector", fn, wait=wait, inc="ve")

        def sc(fn, wait=()):
            return p.emit("scalar", fn, wait=wait, inc="sc")

        def bw(i):
            return [bankfree[i]] if bankfree[i] else []

        dma(ident[:], ident_in, "const")
        dma(cpool[:], cpool_in, "const")
        for v in range(3):
            dma(masks[:, 320 * v:320 * (v + 1)], masks_in[v], "const")
        dma(xh_tm[:], xm[0:HALO, :], "const")
        dma(xg_tm[0:64, :], xg_in, "const")
        dma(fbias[:], bias_in, "const")
        T_CONST = p.ctr["const"]
        ve(lambda E: E.memset(ones1[:], 1.0))

        def fm_transpose_pro(src_tm, dst, n_tok, wait_extra):
            tv = None
            for g in range(2):
                bank = 6 + g
                tk = None
                for c in range(4 * g, 4 * g + 4):
                    tk = pe(lambda E, c=c, bank=bank: E.transpose(
                        pb[bank][:, n_tok * (c % 4):n_tok * (c % 4) + n_tok],
                        src_tm[0:n_tok, 128 * c:128 * c + 128],
                        ident[0:n_tok, 0:n_tok]),
                        wait=wait_extra + bw(bank))
                tv = ve(lambda E, g=g, bank=bank: E.tensor_copy(
                    dst[:, 4 * n_tok * g:4 * n_tok * (g + 1)],
                    pb[bank][:, 0:4 * n_tok]),
                    wait=[("pe", tk)])
                bankfree[bank] = ("ve", tv)
            return tv

        t_xhfm = fm_transpose_pro(xh_tm, xh_fm, 128, [("const", T_CONST)])
        t_xgfm = fm_transpose_pro(xg_tm, xg_fm, 64, [("const", T_CONST)])

        wcol_used = {}
        t_kp = None
        tk = None
        for k in range(NCH):
            buf = wcol[k % 2]
            wsem = f"wc{k % 2}"
            wb = [("pe", wcol_used[k - 2])] if k >= 2 else []
            tl = dma(buf[:], wqkv[128 * k:128 * (k + 1), 1024:2048], wsem, wait=wb)
            for m in range(8):
                tk = pe(lambda E, k=k, m=m, buf=buf: E.matmul(
                    pb[m][:, 0:128], buf[:, 128 * m:128 * m + 128],
                    xh_fm[:, 128 * k:128 * k + 128],
                    start=(k == 0), stop=False),
                    wait=[(wsem, tl), ("ve", t_xhfm)] + (bw(m) if k == 0 else []))
            for m in range(8):
                tk = pe(lambda E, k=k, m=m, buf=buf: E.matmul(
                    pb[m][:, 128:192], buf[:, 128 * m:128 * m + 128],
                    xg_fm[:, 64 * k:64 * k + 64],
                    start=False, stop=(k == NCH - 1)),
                    wait=[("ve", t_xgfm)])
            wcol_used[k] = tk
        for m in range(8):
            t_kp = ve(lambda E, m=m: E.tensor_copy(
                kwin[:, KW * m + 64:KW * m + 192], pb[m][:, 0:128]),
                wait=[("pe", tk)])
            t_kp = ve(lambda E, m=m: E.tensor_copy(
                kwin[:, KW * m:KW * m + 64], pb[m][:, 128:192]),
                wait=[("pe", tk)])
            bankfree[m] = ("ve", t_kp)
        wcol_prev_last2 = [wcol_used[6], wcol_used[7]]

        wrow_used = {}
        tkh = tkg2 = None
        for k in range(NCH):
            buf = wrow[k % 3]
            wsem = f"wr{k % 3}"
            wb = [("pe", wrow_used[k - 3])] if k >= 3 else []
            tl = dma(buf[:], wqkv[128 * k:128 * (k + 1), 2048:3072], wsem, wait=wb)
            for half in range(2):
                tkh = pe(lambda E, k=k, half=half, buf=buf: E.matmul(
                    pb[2 + half][:], xh_fm[:, 128 * k:128 * k + 128],
                    buf[:, 512 * half:512 * half + 512],
                    start=(k == 0), stop=(k == NCH - 1)),
                    wait=[(wsem, tl), ("ve", t_xhfm)] + (bw(2 + half) if k == 0 else []))
            for half in range(2):
                tkg2 = pe(lambda E, k=k, half=half, buf=buf: E.matmul(
                    pb[4 + half][0:64, :], xg_fm[:, 64 * k:64 * k + 64],
                    buf[:, 512 * half:512 * half + 512],
                    start=(k == 0), stop=(k == NCH - 1)),
                    wait=[("ve", t_xgfm)] + (bw(4 + half) if k == 0 else []))
            wrow_used[k] = tkg2
        wrow_prev_last3 = [wrow_used[6], wrow_used[7], wrow_used[5]]
        t_vp = t_vg = None
        for half in range(2):
            t_vp = ve(lambda E, half=half: E.tensor_copy(
                vprev[:, 512 * half:512 * half + 512], pb[2 + half][:]),
                wait=[("pe", tkh)])
            bankfree[2 + half] = ("ve", t_vp)
        for half in range(2):
            t_vg = ve(lambda E, half=half: E.tensor_copy(
                vglob[0:64, 512 * half:512 * half + 512], pb[4 + half][0:64, :]),
                wait=[("pe", tkg2)])
            bankfree[4 + half] = ("ve", t_vg)
        prev_tok = max(t_kp, t_vp)

        t_wfin = None
        t_wb_cons = [None, None]
        for dst, asrc, bsrc in ((wtr, aer_in, ber_in), (wti, aei_in, bei_in)):
            for k in range(4):
                for c in range(4):
                    wa = [("gp", t_wb_cons[0])] if t_wb_cons[0] else []
                    wb_ = [("gp", t_wb_cons[1])] if t_wb_cons[1] else []
                    ta = gdma(wbuf[0][:], asrc[c, 128 * k:128 * k + 128, :],
                              "wbA", wait=wa)
                    tb = gdma(wbuf[1][:], bsrc[c, 128 * k:128 * k + 128, :],
                              "wbB", wait=wb_)
                    if c == 0:
                        t_wfin = gve(lambda E, dst=dst, k=k: E.tensor_tensor(
                            dst[:, 512 * k:512 * (k + 1)],
                            wbuf[0][:], wbuf[1][:], ALU.mult),
                            wait=[("wbA", ta), ("wbB", tb)])
                    else:
                        t_wfin = gve(lambda E: E.tensor_tensor(
                            prodscr[:, 0:512], wbuf[0][:], wbuf[1][:], ALU.mult),
                            wait=[("wbA", ta), ("wbB", tb)])
                        t_wfin = gve(lambda E, dst=dst, k=k: E.tensor_tensor(
                            dst[:, 512 * k:512 * (k + 1)],
                            dst[:, 512 * k:512 * (k + 1)], prodscr[:, 0:512],
                            ALU.add))
                    t_wb_cons = [t_wfin, t_wfin]
        for k in range(4):
            t_wfin = gve(lambda E, k=k: E.tensor_scalar(
                wtn[:, 512 * k:512 * (k + 1)], wti[:, 512 * k:512 * (k + 1)],
                cpool[:, C_NEG1:C_NEG1 + 1], None, ALU.mult),
                wait=[("const", T_CONST)])

        t_xload = {}
        for sub in range(NSUB):
            wpro = [("ve", t_xgfm)] if sub < 2 else []
            t_xload[(0, sub)] = dma(x_tm[sub][:],
                                    xm[HALO + 128 * sub:HALO + 128 * (sub + 1), :],
                                    f"x{sub}", wait=wpro)

        if dbg:
            store(dbg_t["dbg_wtr"], wtr[:], wait=[("gp", t_wfin)])
            store(dbg_t["dbg_wti"], wti[:], wait=[("gp", t_wfin)])
            store(dbg_t["dbg_vglob"], vglob[0:64, :], wait=[("ve", t_vg)])
            store(dbg_t["dbg_vprev"], vprev[:], wait=[("ve", t_vp)])

        t_sqscr_rd = None
        t_attn_done_prev = None
        t_proj_done_prev = None
        t_ffn_done_prev = None
        t_resid = {}
        t_store = {}
        t_ln1nrm_all = {}
        t_ln2nrm_all = {}
        wout_prev_last3 = wrow_prev_last3

        for t in range(nt):
            xb = [x_tm[s] for s in range(NSUB)]

            w_ffnprev = [("pe", t_ffn_done_prev)] if t_ffn_done_prev else []
            t_xT = None
            for sub in range(NSUB):
                for g in range(2):
                    bank = 6 + g
                    tk = None
                    for c in range(4 * g, 4 * g + 4):
                        tk = pe(lambda E, c=c, bank=bank, x=xb[sub]: E.transpose(
                            pb[bank][:, 128 * (c % 4):128 * (c % 4) + 128],
                            x[:, 128 * c:128 * c + 128], ident[:]),
                            wait=[(f"x{sub}", t_xload[(t, sub)])] + bw(bank))
                    for c in range(4 * g, 4 * g + 4):
                        t_xT = ve(lambda E, c=c, sub=sub, bank=bank: E.tensor_copy(
                            xT[:, 512 * c + 128 * sub:512 * c + 128 * sub + 128],
                            pb[bank][:, 128 * (c % 4):128 * (c % 4) + 128]),
                            wait=[("pe", tk)] + w_ffnprev)
                    bankfree[bank] = ("ve", t_xT)

            w_attnprev = [("pe", t_attn_done_prev)] if t_attn_done_prev else []
            t_q_tok = {}
            t_k_tok = {}
            for part in range(2):
                tk = None
                for k in range(NCH):
                    buf = wcol[k % 2]
                    wsem = f"wc{k % 2}"
                    li = 2 * part + (k % 2)
                    wb = [("pe", wcol_used[k - 2])] if k >= 2 else \
                         [("pe", wcol_prev_last2[k])]
                    tl = dma(buf[:],
                             wqkv[128 * k:128 * (k + 1),
                                  1024 * part:1024 * part + 1024], wsem, wait=wb)
                    for m in range(8):
                        tk = pe(lambda E, k=k, m=m, buf=buf: E.matmul(
                            pb[m][:], buf[:, 128 * m:128 * m + 128],
                            xT[:, 512 * k:512 * (k + 1)],
                            start=(k == 0), stop=(k == NCH - 1)),
                            wait=[(wsem, tl), ("ve", t_xT)] +
                                 (bw(m) if k == 0 else []))
                    wcol_used[k] = tk
                wcol_prev_last2 = [wcol_used[6], wcol_used[7]]
                if part == 0:
                    for m in range(8):
                        tq = sc(lambda E, m=m: E.activation(
                            qfm[:, 512 * m:512 * (m + 1)], pb[m][:],
                            AF.Copy, scale=0.125),
                            wait=[("pe", tk)] + w_attnprev)
                        t_q_tok[m] = tq
                        bankfree[m] = ("sc", tq)
                else:
                    for m in range(8):
                        tkk = ve(lambda E, m=m: E.tensor_copy(
                            kwin[:, KW * m + 192:KW * m + KW], pb[m][:]),
                            wait=[("pe", tk), ("ve", prev_tok)] + w_attnprev)
                        t_k_tok[m] = tkk
                        bankfree[m] = ("ve", tkk)

            t_v_tok = {}
            for k in range(NCH):
                buf = wrow[k % 3]
                wsem = f"wr{k % 3}"
                if k >= 3:
                    wb = [("pe", wrow_used[k - 3])]
                elif t == 0:
                    wb = [("pe", wrow_prev_last3[k])]
                else:
                    wb = [("pe", wout_prev_last3[k])]
                tl = dma(buf[:], wqkv[128 * k:128 * (k + 1), 2048:3072], wsem, wait=wb)
                tk = None
                for s in range(NSUB):
                    for half in range(2):
                        bank = 2 * s + half
                        tk = pe(lambda E, k=k, s=s, half=half, buf=buf, bank=bank:
                                E.matmul(
                                    pb[bank][:],
                                    xT[:, 512 * k + 128 * s:512 * k + 128 * s + 128],
                                    buf[:, 512 * half:512 * half + 512],
                                    start=(k == 0), stop=(k == NCH - 1)),
                                wait=[(wsem, tl), ("ve", t_xT)] +
                                     (bw(bank) if k == 0 else []))
                wrow_used[k] = tk
            for s in range(NSUB):
                for half in range(2):
                    bank = 2 * s + half
                    tvv = ve(lambda E, s=s, half=half, bank=bank: E.tensor_copy(
                        vcur[s][:, 512 * half:512 * half + 512], pb[bank][:]),
                        wait=[("pe", wrow_used[NCH - 1]), ("ve", prev_tok)] +
                             w_attnprev +
                             ([("gp", t_wfin)] if (t == 0 and s == 0) else []))
                    t_v_tok[s] = tvv
                    bankfree[bank] = ("ve", tvv)

            w_projprev = [("pe", t_proj_done_prev)] if t_proj_done_prev else []
            seq = [(pair, r, hh) for pair in range(8) for r in range(NSUB)
                   for hh in range(2)]
            NIT = len(seq)
            DEPTH = 2
            tok_exp = {}
            tok_nm = {}
            tok_trans = {}
            tok_tpe = {}
            tok_av = {}
            t_ao_tok = {}
            t_attn_done = None

            def emit_head(i):
                pair, r, hh = seq[i]
                midx = min(4 * t + r, 2)
                i3 = i % 3
                sb = i % 3
                q_ap = qfm[64 * hh:64 * hh + 64,
                           512 * pair + 128 * r:512 * pair + 128 * r + 128]
                wsc = [("sc", t_q_tok[pair]), ("ve", t_k_tok[pair]),
                       ("ve", prev_tok)] + bw(sb)
                if r == 0:
                    tk1 = pe(lambda E, q_ap=q_ap, hh=hh, pair=pair, sb=sb: E.matmul(
                        pb[sb][:, 0:320], q_ap,
                        kwin[64 * hh:64 * hh + 64, KW * pair:KW * pair + 320],
                        start=True, stop=True), wait=wsc)
                else:
                    tk1 = pe(lambda E, q_ap=q_ap, hh=hh, pair=pair, sb=sb: E.matmul(
                        pb[sb][:, 0:64], q_ap,
                        kwin[64 * hh:64 * hh + 64, KW * pair:KW * pair + 64],
                        start=True, stop=True), wait=wsc)
                    tk1 = pe(lambda E, q_ap=q_ap, hh=hh, pair=pair, r=r, sb=sb:
                             E.matmul(
                                 pb[sb][:, 64:320], q_ap,
                                 kwin[64 * hh:64 * hh + 64,
                                      KW * pair + 64 + 128 * r:
                                      KW * pair + 64 + 128 * r + 256],
                                 start=True, stop=True))
                tmx = ve(lambda E, sb=sb, i3=i3: E.reduce_max(
                    smax[i3][:], pb[sb][:, 0:320], axis=AX.X, negate=True),
                    wait=[("pe", tk1)] +
                         ([("sc", tok_exp[i - 3])] if i - 3 in tok_exp else []))
                tex = sc(lambda E, sb=sb, i3=i3: E.activation(
                    pexp[i3][:], pb[sb][:, 0:320], AF.Exp,
                    bias=smax[i3][:], scale=1.0),
                    wait=[("ve", tmx)] +
                         ([("pe", tok_trans[i - 3])] if i - 3 in tok_trans else []))
                tok_exp[i] = tex
                bankfree[sb] = ("sc", tex)
                tms = ve(lambda E, i3=i3, midx=midx: E.scalar_tensor_tensor(
                    pexp[i3][:], pexp[i3][:], cpool[:, C_ONE:C_ONE + 1],
                    masks[:, 320 * midx:320 * (midx + 1)],
                    ALU.mult, ALU.mult, accum_out=ssum[i3][:]),
                    wait=[("sc", tex), ("const", T_CONST)])
                trec = ve(lambda E, i3=i3: E.reciprocal(
                    srinv[i3][:], ssum[i3][:]), wait=[("ve", tms)])
                tnm = ve(lambda E, i3=i3: E.tensor_scalar(
                    pexp[i3][:], pexp[i3][:], srinv[i3][:], None, ALU.mult),
                    wait=[("ve", trec)])
                tok_nm[i] = tnm

            def emit_trans(j):
                pair, r, hh = seq[j]
                j3 = j % 3
                tb = 3 if j % 2 == 0 else 6
                tk2 = pe(lambda E, j3=j3, tb=tb: E.transpose(
                    pb[tb][0:64, 0:128], pexp[j3][:, 0:64], ident[:]),
                    wait=[("ve", tok_nm[j])] + bw(tb))
                tk2 = pe(lambda E, j3=j3, tb=tb: E.transpose(
                    pb[tb][:, 128:256], pexp[j3][:, 64:192], ident[:]))
                tk2 = pe(lambda E, j3=j3, tb=tb: E.transpose(
                    pb[tb][:, 256:384], pexp[j3][:, 192:320], ident[:]))
                tok_trans[j] = tk2
                tpe = sc(lambda E, j3=j3, j2=j % 2, tb=tb: E.copy(
                    pT_sb[j2][:], pb[tb][:, 0:384]),
                    wait=[("pe", tk2)] +
                         ([("pe", tok_av[j - 2])] if j - 2 in tok_av else []))
                tok_tpe[j] = tpe
                bankfree[tb] = ("sc", tpe)

            def emit_tail(j):
                nonlocal t_attn_done
                pair, r, hh = seq[j]
                h = 2 * pair + hh
                av_bank = 4 + (pair % 2)
                tpe = tok_tpe[j]
                tp_av = (0, 64) if hh else None
                wav = [("sc", tpe), ("ve", t_v_tok[r]), ("ve", t_vg),
                       ("ve", t_vp)] + \
                      ([("ve", t_v_tok[r - 1])] if r >= 1 else []) + \
                      (bw(av_bank) if (r == 0 and hh == 0) else [])
                vband = vprev if r == 0 else vcur[r - 1]
                j2 = j % 2
                tk3 = pe(lambda E, h=h, hh=hh, r=r, j2=j2, ab=av_bank,
                         tp=tp_av: E.matmul(
                             pb[ab][64 * hh:64 * hh + 64, 128 * r:128 * r + 128],
                             vglob[0:64, 64 * h:64 * h + 64],
                             pT_sb[j2][0:64, 0:128],
                             start=True, stop=False, tile_position=tp),
                         wait=wav)
                tk3 = pe(lambda E, h=h, hh=hh, r=r, j2=j2, ab=av_bank,
                         vb=vband, tp=tp_av: E.matmul(
                             pb[ab][64 * hh:64 * hh + 64, 128 * r:128 * r + 128],
                             vb[:, 64 * h:64 * h + 64],
                             pT_sb[j2][:, 128:256],
                             start=False, stop=False, tile_position=tp))
                tk3 = pe(lambda E, h=h, hh=hh, r=r, j2=j2, ab=av_bank,
                         tp=tp_av: E.matmul(
                             pb[ab][64 * hh:64 * hh + 64, 128 * r:128 * r + 128],
                             vcur[r][:, 64 * h:64 * h + 64],
                             pT_sb[j2][:, 256:384],
                             start=False, stop=True, tile_position=tp))
                tok_av[j] = tk3
                t_attn_done = tk3
                if r == NSUB - 1 and hh == 1:
                    tao = ve(lambda E, pair=pair, ab=av_bank: E.tensor_copy(
                        aoT[:, 512 * pair:512 * (pair + 1)], pb[ab][:]),
                        wait=[("pe", tk3)] + w_projprev)
                    t_ao_tok[pair] = tao
                    bankfree[av_bank] = ("ve", tao)

            for step in range(NIT + 3):
                if step < NIT:
                    emit_head(step)
                if DEPTH <= step < NIT + DEPTH:
                    emit_trans(step - DEPTH)
                if step >= 3:
                    emit_tail(step - 3)

            if t + 1 < nt:
                tpc = None
                for c in range(NCH):
                    tpc = ve(lambda E, c=c: E.tensor_copy(
                        kwin[:, KW * c + 64:KW * c + 192],
                        kwin[:, KW * c + 192 + 384:KW * c + KW]),
                        wait=[("pe", t_attn_done)])
                tpv = ve(lambda E: E.tensor_copy(vprev[:], vcur[3][:]),
                         wait=[("pe", t_attn_done)])
                prev_tok = tpv

            t_wout_load = {}
            t_wout_used = {}
            t_proj_bank = {}
            for k in range(NCH):
                buf = wrow[k % 3]
                wsem = f"wr{k % 3}"
                if k >= 3:
                    wb = [("pe", t_wout_used[k - 3])]
                else:
                    wb = [("pe", wrow_used[{0: 6, 1: 7, 2: 5}[k]])]
                tl = dma(buf[:], wout[128 * k:128 * (k + 1), :], wsem, wait=wb)
                t_wout_load[k] = tl
                tk = None
                for r in range(NSUB):
                    for half in range(2):
                        bank = 2 * r + half
                        tk = pe(lambda E, k=k, r=r, half=half, buf=buf, bank=bank:
                                E.matmul(
                                    pb[bank][:],
                                    aoT[:, 512 * k + 128 * r:512 * k + 128 * r + 128],
                                    buf[:, 512 * half:512 * half + 512],
                                    start=(k == 0), stop=(k == NCH - 1)),
                                wait=[(wsem, tl), ("ve", t_ao_tok[k])] +
                                     (bw(bank) if k == 0 else []))
                        if k == NCH - 1:
                            t_proj_bank[(r, half)] = tk
                t_wout_used[k] = tk
            wout_prev_last3 = [t_wout_used[6], t_wout_used[7], t_wout_used[5]]
            t_proj_done = t_wout_used[NCH - 1]
            t_ln1pre = {}
            for r in range(NSUB):
                tres = None
                for half in range(2):
                    bank = 2 * r + half
                    tres = ve(lambda E, r=r, half=half, bank=bank, x=xb[r]:
                              E.scalar_tensor_tensor(
                                  ln1tm[r][:, 512 * half:512 * half + 512],
                                  pb[bank][:], cpool[:, C_ONE:C_ONE + 1],
                                  x[:, 512 * half:512 * half + 512],
                                  ALU.mult, ALU.add),
                              wait=[("pe", t_proj_bank[(r, half)])])
                    bankfree[bank] = ("ve", tres)
                t_ln1pre[r] = tres
            t_resid[t] = t_ln1pre[NSUB - 1]
            if t + 1 < nt:
                for sub in range(NSUB):
                    t_xload[(t + 1, sub)] = dma(
                        x_tm[sub][:],
                        xm[HALO + 512 * (t + 1) + 128 * sub:
                           HALO + 512 * (t + 1) + 128 * (sub + 1), :],
                        f"x{sub}", wait=[("ve", t_ln1pre[sub])])

            def ln_stats(si_, inp_fn, pre_wait, wpar):
                nonlocal t_sqscr_rd
                tsqr = sc(lambda E: E.activation(
                    sqscr[:], inp_fn(), AF.Square),
                    wait=pre_wait + wpar +
                         ([("ve", t_sqscr_rd)] if t_sqscr_rd else []))
                tsq = ve(lambda E, si_=si_: E.reduce_sum(
                    s_sq[si_][:], sqscr[:], axis=AX.X), wait=[("sc", tsqr)])
                t_sqscr_rd = tsq
                tred = ve(lambda E, si_=si_: E.reduce_sum(
                    s_sum[si_][:], inp_fn(), axis=AX.X))
                tt = ve(lambda E, si_=si_: E.tensor_scalar(
                    s_mean[si_][:], s_sum[si_][:], cpool[:, C_INV1024:C_INV1024 + 1],
                    None, ALU.mult), wait=[("const", T_CONST), ("ve", tred)])
                tt = ve(lambda E, si_=si_: E.tensor_scalar(
                    s_msq[si_][:], s_sq[si_][:], cpool[:, C_INV1024:C_INV1024 + 1],
                    None, ALU.mult), wait=[("ve", tsq), ("ve", tt)])
                tt = ve(lambda E, si_=si_: E.tensor_tensor(
                    s_var[si_][:], s_mean[si_][:], s_mean[si_][:], ALU.mult),
                    wait=[("ve", tt)])
                tt = ve(lambda E, si_=si_: E.tensor_tensor(
                    s_var[si_][:], s_msq[si_][:], s_var[si_][:], ALU.subtract),
                    wait=[("ve", tt)])
                tvar = ve(lambda E, si_=si_: E.tensor_scalar(
                    s_var[si_][:], s_var[si_][:], cpool[:, C_EPS:C_EPS + 1],
                    None, ALU.add), wait=[("ve", tt)])
                tsd = sc(lambda E, si_=si_: E.activation(
                    s_sd[si_][:], s_var[si_][:], AF.Sqrt), wait=[("ve", tvar)])
                trc = ve(lambda E, si_=si_: E.reciprocal(
                    s_rs[si_][:], s_sd[si_][:]), wait=[("sc", tsd)])
                tsh = ve(lambda E, si_=si_: E.scalar_tensor_tensor(
                    s_sh[si_][:], s_mean[si_][:], cpool[:, C_NEG1:C_NEG1 + 1],
                    s_rs[si_][:], ALU.mult, ALU.mult), wait=[("ve", trc)])
                return tsh

            t_ln1nrm = {}
            t_ln2nrm = {}
            t_ffn_done = None
            for r in range(NSUB):
                si_ = r % 2
                if (r - 2) in t_ln1nrm:
                    wpar = [("ve", t_ln1nrm[r - 2])]
                elif t > 0:
                    wpar = [("ve", t_ln1nrm_all[t - 1][r + 2])]
                else:
                    wpar = []
                tsh = ln_stats(si_, (lambda r=r: ln1tm[r][:]),
                               [("ve", t_ln1pre[r])], wpar)
                tnm = ve(lambda E, r=r, si_=si_: E.tensor_scalar(
                    ln1tm[r][:], ln1tm[r][:], s_rs[si_][:], s_sh[si_][:],
                    ALU.mult, ALU.add), wait=[("ve", tsh)])
                t_ln1nrm[r] = tnm
            for r in range(NSUB):
                tnm = t_ln1nrm[r]
                t_ln1fm_r = None
                for g in range(2):
                    bank = 2 + g
                    tk = None
                    for c in range(4 * g, 4 * g + 4):
                        tk = pe(lambda E, c=c, r=r, bank=bank: E.transpose(
                            pb[bank][:, 128 * (c % 4):128 * (c % 4) + 128],
                            ln1tm[r][:, 128 * c:128 * c + 128], ident[:]),
                            wait=[("ve", tnm)] + bw(bank))
                    for c in range(4 * g, 4 * g + 4):
                        t_ln1fm_r = ve(lambda E, c=c, r=r, bank=bank: E.tensor_copy(
                            xT[:, 512 * c + 128 * r:512 * c + 128 * r + 128],
                            pb[bank][:, 128 * (c % 4):128 * (c % 4) + 128]),
                            wait=[("pe", tk)])
                    bankfree[bank] = ("ve", t_ln1fm_r)
                banks = (4, 5) if r % 2 == 0 else (6, 7)
                tk = None
                for c in range(4):
                    lhs = xT[:, 512 * c + 128 * r:512 * c + 128 * r + 128]
                    tk = pe(lambda E, lhs=lhs, c=c, bank=banks[0]: E.matmul(
                        pb[bank][:], lhs, wtr[:, 512 * c:512 * (c + 1)],
                        start=(c == 0), stop=False),
                        wait=[("ve", t_ln1fm_r), ("gp", t_wfin)] +
                             (bw(banks[0]) if c == 0 else []))
                    tk = pe(lambda E, lhs=lhs, c=c, bank=banks[1]: E.matmul(
                        pb[bank][:], lhs, wti[:, 512 * c:512 * (c + 1)],
                        start=(c == 0), stop=False),
                        wait=(bw(banks[1]) if c == 0 else []))
                for c in range(4):
                    lhs = xT[:, 512 * (4 + c) + 128 * r:512 * (4 + c) + 128 * r + 128]
                    tk = pe(lambda E, lhs=lhs, c=c, bank=banks[0]: E.matmul(
                        pb[bank][:], lhs, wtn[:, 512 * c:512 * (c + 1)],
                        start=False, stop=False))
                    tk = pe(lambda E, lhs=lhs, c=c, bank=banks[1]: E.matmul(
                        pb[bank][:], lhs, wtr[:, 512 * c:512 * (c + 1)],
                        start=False, stop=False))
                tk = pe(lambda E, bank=banks[0]: E.matmul(
                    pb[bank][:], ones1[:], fbias[:, 0:512],
                    start=False, stop=True), wait=[("const", T_CONST)])
                tk = pe(lambda E, bank=banks[1]: E.matmul(
                    pb[bank][:], ones1[:], fbias[:, 512:1024],
                    start=False, stop=True))
                t_ffn_done = tk
                ob = otm[r % 2]
                stsem = f"st{r % 2}"
                if (t, r - 2) in t_store:
                    wst = [(stsem, t_store[(t, r - 2)])]
                elif (t - 1, r + 2) in t_store:
                    wst = [(stsem, t_store[(t - 1, r + 2)])]
                else:
                    wst = []
                tpre = None
                for half in range(2):
                    bank = banks[half]
                    tpre = ve(lambda E, r=r, half=half, bank=bank, ob=ob:
                              E.scalar_tensor_tensor(
                                  ob[:, 512 * half:512 * half + 512],
                                  pb[bank][:], cpool[:, C_ONE:C_ONE + 1],
                                  ln1tm[r][:, 512 * half:512 * half + 512],
                                  ALU.mult, ALU.add),
                              wait=[("pe", tk)] + wst)
                    bankfree[bank] = ("ve", tpre)
                si2 = 2 + r % 2
                if (r - 2) in t_ln2nrm:
                    wpar = [("ve", t_ln2nrm[r - 2])]
                elif t > 0:
                    wpar = [("ve", t_ln2nrm_all[t - 1][r + 2])]
                else:
                    wpar = []
                tsh = ln_stats(si2, (lambda ob=ob: ob[:]), [("ve", tpre)], wpar)
                tnm = ve(lambda E, ob=ob, si2=si2: E.tensor_scalar(
                    ob[:], ob[:], s_rs[si2][:], s_sh[si2][:],
                    ALU.mult, ALU.add), wait=[("ve", tsh)])
                t_ln2nrm[r] = tnm
                t_store[(t, r)] = store(
                    y[512 * t + 128 * r:512 * t + 128 * (r + 1), :], ob[:],
                    sem=stsem, wait=[("ve", tnm)])
            t_ln1nrm_all[t] = t_ln1nrm
            t_ln2nrm_all[t] = t_ln2nrm

            if dbg and t == 0:
                store(dbg_t["dbg_qfm"], qfm[:], wait=[("sc", t_q_tok[7])])
                store(dbg_t["dbg_kwin"], kwin[:], wait=[("ve", t_k_tok[7])])
                for s in range(NSUB):
                    store(dbg_t["dbg_vcur"][s], vcur[s][:], wait=[("ve", t_v_tok[s])])
                store(dbg_t["dbg_aot"], aoT[:], wait=[("ve", t_ao_tok[7])])
                for r in range(NSUB):
                    store(dbg_t["dbg_ln1"][r], ln1tm[r][:],
                          wait=[("ve", t_ln1nrm[r])])
                store(dbg_t["dbg_xt"], xT[:], wait=[("ve", t_ln1fm_r)])

            t_attn_done_prev = t_attn_done
            t_proj_done_prev = t_proj_done
            t_ffn_done_prev = t_ffn_done

        p.emit("gpsimd", None,
               wait=[(s, p.ctr[s]) for s in ("st0", "st1", "dbgst")
                     if p.ctr[s] > 0])

        globals()["_last_plan"] = dict(p.ctr)

        with nc.Block() as block:
            def mk(eng_name):
                def body(E):
                    for waits, fn, inc, amt in p.ops[eng_name]:
                        for s, v in waits:
                            E.wait_ge(sems[s], v)
                        if fn is not None:
                            inst = fn(E)
                            if inc:
                                inst.then_inc(sems[inc], amt)
                return body
            block.sync(mk("sync"))
            block.tensor(mk("tensor"))
            block.vector(mk("vector"))
            block.scalar(mk("scalar"))
            block.gpsimd(mk("gpsimd"))

    return nc



def make_core_inputs(x, w_qkv, w_out, G1_R, G2_R, G1_I, G2_I, b_r, b_i):
    x = np.ascontiguousarray(np.asarray(x, dtype=np.float32))
    w_qkv = np.ascontiguousarray(np.asarray(w_qkv, dtype=np.float32))
    w_out = np.ascontiguousarray(np.asarray(w_out, dtype=np.float32))
    r_idx = np.arange(128)[:, None]
    c_idx = np.arange(128)[None, :]
    low = (c_idx <= r_idx).astype(np.float32)
    up = (c_idx >= r_idx).astype(np.float32)
    up64 = ((c_idx >= r_idx) | (c_idx < 64)).astype(np.float32)
    g0 = np.zeros((128, 64), np.float32)
    g1 = np.ones((128, 64), np.float32)
    z128 = np.zeros((128, 128), np.float32)
    masks_h0 = np.ascontiguousarray(np.stack([
        np.concatenate([g0, z128, low], axis=1),
        np.concatenate([g0, up64, low], axis=1),
        np.concatenate([g1, up, low], axis=1)]))
    m2 = np.concatenate([g1, up, low], axis=1)
    masks_h1 = np.ascontiguousarray(np.stack([m2, m2, m2]))
    ident = np.eye(128, dtype=np.float32)
    cpool = np.zeros((128, 8), np.float32)
    cpool[:, C_INV1024] = 1.0 / 1024.0
    cpool[:, C_EPS] = 1e-5
    cpool[:, C_NEG1] = -1.0
    cpool[:, C_ONE] = 1.0

    oi = np.arange(512)
    aexp = {}
    bexp = {}
    for tag, G1, G2 in (("r", G1_R, G2_R), ("i", G1_I, G2_I)):
        G1 = np.asarray(G1, np.float32)
        G2 = np.asarray(G2, np.float32)
        a = np.empty((4, 512, 512), np.float32)
        b = np.empty((4, 512, 512), np.float32)
        for c in range(4):
            A = G1[0, :, c, :]
            Bm = G2[c, :, 0, :]
            a[c] = A[oi // 16, :][:, oi // 16].T
            b[c] = Bm[oi % 16, :][:, oi % 16].T
        aexp[tag] = np.ascontiguousarray(a)
        bexp[tag] = np.ascontiguousarray(b)
    fbias = np.ascontiguousarray(
        np.concatenate([np.asarray(b_r, np.float32),
                        np.asarray(b_i, np.float32)])[None, :])

    in_maps = []
    for core in range(8):
        b = core // 2
        half = core % 2
        if half == 0:
            x_main = np.concatenate([np.zeros((HALO, D), np.float32),
                                     x[b, 0:OWN]], axis=0)
        else:
            x_main = x[b, OWN - HALO:SEQ]
        in_maps.append({
            "x_main": np.ascontiguousarray(x_main),
            "x_glob": np.ascontiguousarray(x[b, 0:GLOB]),
            "w_qkv": w_qkv,
            "w_out": w_out,
            "masks": masks_h0 if half == 0 else masks_h1,
            "ident": ident,
            "cpool": cpool,
            "aexp_r": aexp["r"], "bexp_r": bexp["r"],
            "aexp_i": aexp["i"], "bexp_i": bexp["i"],
            "fbias": fbias,
        })
    return in_maps


def kernel(L0_attn_x, L0_attn_w_qkv, L0_attn_w_out, L0_ffn_G1_R, L0_ffn_G2_R,
           L0_ffn_G1_I, L0_ffn_G2_I, L0_ffn_bias_real, L0_ffn_bias_imag):
    in_maps = make_core_inputs(L0_attn_x, L0_attn_w_qkv, L0_attn_w_out,
                               L0_ffn_G1_R, L0_ffn_G2_R, L0_ffn_G1_I,
                               L0_ffn_G2_I, L0_ffn_bias_real, L0_ffn_bias_imag)
    nc = build_program(dbg=False, nt=NT)
    res = run_bass_kernel_spmd(nc, in_maps, core_ids=list(range(8)))
    out = np.empty((NB, SEQ, D), dtype=np.float32)
    for core in range(8):
        b, half = core // 2, core % 2
        out[b, OWN * half:OWN * (half + 1)] = res.results[core]["y"]
    return out
